# version 01
# baseline (speedup 1.0000x reference)
import math
import numpy as np
import ml_dtypes
import concourse.bass as bass
import concourse.mybir as mybir
from concourse.bass_utils import run_bass_kernel_spmd

F32 = mybir.dt.float32
BF16 = mybir.dt.bfloat16
AF = mybir.ActivationFunctionType
ALU = mybir.AluOpType
AX = mybir.AxisListType

BF = ml_dtypes.bfloat16
MAGIC = 12582912.0

B, S, D = 4, 4096, 1024
H, DK = 16, 64
SB = S // 128
QL = 16
TOK = QL * 128
E, DFF = 8, 2048
INNER = 4096
N_CORES = 8



def _own_token_idx(par):
    return np.concatenate(
        [np.arange(128 * (2 * ql + par), 128 * (2 * ql + par) + 128) for ql in range(QL)]
    )


def _rope_tables():
    theta = 1.0 / (10000.0 ** (np.arange(0, DK, 2, dtype=np.float64) / DK))
    m = np.arange(S, dtype=np.float64)
    ang = np.outer(m, theta)
    ang = np.concatenate([ang, ang], axis=-1)
    return np.cos(ang).astype(np.float32), np.sin(ang).astype(np.float32)


def _cs(a):
    return np.ascontiguousarray(a)


def _t(a):
    return np.ascontiguousarray(np.asarray(a).T)


def new_nc():
    return bass.Bass("TRN2", target_bir_lowering=False, debug=False)


def fence(e, sem, cnt, inst):
    cnt[0] += 1
    inst.then_inc(sem, 1)
    e.wait_ge(sem, cnt[0])



def build_L0A():
    nc = new_nc()
    xk = nc.dram_tensor("xk", [S, D], F32, kind="ExternalInput").ap()
    xn = nc.dram_tensor("xn", [S, D], BF16, kind="ExternalOutput").ap()
    xn32o = nc.dram_tensor("xn32o", [TOK, D], F32, kind="ExternalOutput").ap()

    NH, NJ = 2, 16
    xs = nc.alloc_sbuf_tensor("xs", [128, NJ * D], F32).ap()
    xo = nc.alloc_sbuf_tensor("xo", [128, NJ * D], BF16).ap()
    xo32 = nc.alloc_sbuf_tensor("xo32", [128, (NJ // 2) * D], F32).ap()
    sc = nc.alloc_sbuf_tensor("sc", [128, NJ * D], BF16).ap()
    st = nc.alloc_sbuf_tensor("st", [128, 64], F32).ap()

    with (
        nc.semaphore("ld") as ld,
        nc.semaphore("sA") as sA,
        nc.semaphore("sV") as sV,
        nc.semaphore("vch") as vch,
        nc.semaphore("so") as so,
        nc.Block() as block,
    ):
        @block.sync
        def _(e):
            for h in range(NH):
                if h > 0:
                    e.wait_ge(sV, 4 * h)
                for j in range(NJ):
                    c = NJ * h + j
                    e.dma_start(out=xs[:, j * D:(j + 1) * D],
                                in_=xk[128 * c:128 * (c + 1), :]).then_inc(ld, 16)

        @block.scalar
        def _(e):
            for h in range(NH):
                e.wait_ge(ld, 16 * NJ * (h + 1))
                if h > 0:
                    e.wait_ge(sV, 4 * h)
                last = None
                for j in range(NJ):
                    last = e.activation(sc[:, j * D:(j + 1) * D], xs[:, j * D:(j + 1) * D],
                                        AF.Identity, accum_out=st[:, j:j + 1])
                last.then_inc(sA, 1)
                e.wait_ge(sV, 4 * h + 1)
                for j in range(NJ):
                    last = e.activation(sc[:, j * D:(j + 1) * D], xs[:, j * D:(j + 1) * D],
                                        AF.Square, accum_out=st[:, 16 + j:17 + j])
                last.then_inc(sA, 1)
                e.wait_ge(sV, 4 * h + 2)
                e.activation(st[:, 0:16], st[:, 32:48], AF.Sqrt).then_inc(sA, 1)

        @block.vector
        def _(e):
            cnt = [0]
            for h in range(NH):
                e.wait_ge(sA, 3 * h + 1)
                fence(e, vch, cnt,
                      e.tensor_scalar(st[:, 48:64], st[:, 0:16], 1.0 / D, None, ALU.mult))
                last = None
                for j in range(NJ):
                    last = e.tensor_scalar(xs[:, j * D:(j + 1) * D], xs[:, j * D:(j + 1) * D],
                                           st[:, 48 + j:49 + j], None, ALU.subtract)
                last.then_inc(sV, 1)
                e.wait_ge(sA, 3 * h + 2)
                e.tensor_scalar(st[:, 32:48], st[:, 16:32], 1.0 / D, 1e-5, ALU.mult, ALU.add) \
                    .then_inc(sV, 1)
                e.wait_ge(sA, 3 * h + 3)
                fence(e, vch, cnt, e.reciprocal(st[:, 32:48], st[:, 0:16]))
                if h > 0:
                    e.wait_ge(so, 16 * (NJ + NJ // 2) * h)
                for j in range(NJ):
                    last = e.tensor_scalar(xo[:, j * D:(j + 1) * D], xs[:, j * D:(j + 1) * D],
                                           st[:, 32 + j:33 + j], None, ALU.mult)
                last.then_inc(sV, 1)
                for oj, j in enumerate(range(1, NJ, 2)):
                    last = e.tensor_scalar(xo32[:, oj * D:(oj + 1) * D], xs[:, j * D:(j + 1) * D],
                                           st[:, 32 + j:33 + j], None, ALU.mult)
                last.then_inc(sV, 1)

        @block.gpsimd
        def _(e):
            for h in range(NH):
                e.wait_ge(sV, 4 * h + 3)
                for j in range(NJ):
                    c = NJ * h + j
                    e.dma_start(out=xn[128 * c:128 * (c + 1), :],
                                in_=xo[:, j * D:(j + 1) * D]).then_inc(so, 16)
                e.wait_ge(sV, 4 * h + 4)
                for oj in range(NJ // 2):
                    oc = (NJ // 2) * h + oj
                    e.dma_start(out=xn32o[128 * oc:128 * (oc + 1), :],
                                in_=xo32[:, oj * D:(oj + 1) * D]).then_inc(so, 16)
            e.wait_ge(so, 16 * (NJ + NJ // 2) * NH)
    return nc



def build_L0B():
    nc = new_nc()
    xnf = nc.dram_tensor("xnf", [D, S], BF16, kind="ExternalInput").ap()
    wqT = nc.dram_tensor("wqT", [D, D], F32, kind="ExternalInput").ap()
    wkT = nc.dram_tensor("wkT", [D, D], F32, kind="ExternalInput").ap()
    wvT = nc.dram_tensor("wvT", [D, D], F32, kind="ExternalInput").ap()
    cosq = nc.dram_tensor("cosq", [TOK, D], BF16, kind="ExternalInput").ap()
    sinq = nc.dram_tensor("sinq", [TOK, D], BF16, kind="ExternalInput").ap()
    cosk = nc.dram_tensor("cosk", [S, D], BF16, kind="ExternalInput").ap()
    sink = nc.dram_tensor("sink", [S, D], BF16, kind="ExternalInput").ap()
    qo = nc.dram_tensor("qo", [TOK, D], BF16, kind="ExternalOutput").ap()
    ko = nc.dram_tensor("ko", [S, D], BF16, kind="ExternalOutput").ap()
    vo = nc.dram_tensor("vo", [S, D], BF16, kind="ExternalOutput").ap()

    xns = nc.alloc_sbuf_tensor("xns", [128, 8 * S], BF16).ap()
    ws = {nm: nc.alloc_sbuf_tensor(nm, [128, 8 * D], BF16).ap() for nm in ("wq", "wk", "wv")}
    ct = [nc.alloc_sbuf_tensor(f"ct{i}", [128, D], BF16).ap() for i in range(2)]
    sn = [nc.alloc_sbuf_tensor(f"sn{i}", [128, D], BF16).ap() for i in range(2)]
    pt = [nc.alloc_sbuf_tensor(f"pt{i}", [128, D], BF16).ap() for i in range(2)]
    rt = [nc.alloc_sbuf_tensor(f"rt{i}", [128, D], BF16).ap() for i in range(2)]
    tw = [nc.alloc_sbuf_tensor(f"tw{i}", [128, D], BF16).ap() for i in range(2)]
    ps = [nc.alloc_psum_tensor(f"ps{i}", [128, 512], F32).ap() for i in range(4)]

    own_cols = [128 * (2 * ql + 1) for ql in range(QL)]

    jobs = [("q", i) for i in range(QL)] + [("k", i) for i in range(SB)] + \
           [("v", i) for i in range(SB)]
    roped_n = [n for n, (kind, _) in enumerate(jobs) if kind != "v"]

    with (
        nc.semaphore("ld") as ld,
        nc.semaphore("lt0") as lt0,
        nc.semaphore("lt1") as lt1,
        nc.semaphore("mm") as mm,
        nc.semaphore("ev") as ev,
        nc.semaphore("vch") as vch,
        nc.semaphore("so0") as so0,
        nc.semaphore("so1") as so1,
        nc.Block() as block,
    ):
        lt = [lt0, lt1]
        so = [so0, so1]
        @block.sync
        def _(e):
            for c in range(8):
                e.dma_start(out=xns[:, c * S:(c + 1) * S],
                            in_=xnf[128 * c:128 * (c + 1), :]).then_inc(ld, 16)

        @block.gpsimd
        def _(e):
            for nm, w in (("wq", wqT), ("wk", wkT), ("wv", wvT)):
                for c in range(8):
                    e.dma_start(out=ws[nm][:, c * D:(c + 1) * D],
                                in_=w[128 * c:128 * (c + 1), :]).then_inc(ld, 16)

        @block.scalar
        def _(e):
            for rj, n in enumerate(roped_n):
                kind, i = jobs[n]
                tc, tsn = (cosq, sinq) if kind == "q" else (cosk, sink)
                if rj >= 2:
                    e.wait_ge(ev, roped_n[rj - 2] + 1)
                e.dma_start(out=ct[rj % 2][:, :], in_=tc[128 * i:128 * (i + 1), :]).then_inc(lt[rj % 2], 16)
                e.dma_start(out=sn[rj % 2][:, :], in_=tsn[128 * i:128 * (i + 1), :]).then_inc(lt[rj % 2], 16)

        @block.tensor
        def _(e):
            e.wait_ge(ld, 16 * 32)
            for n, (kind, i) in enumerate(jobs):
                w = ws[{"q": "wq", "k": "wk", "v": "wv"}[kind]]
                col = own_cols[i] if kind == "q" else 128 * i
                bb = n % 2
                if n >= 2:
                    e.wait_ge(ev, n - 1)
                for oc in range(2):
                    for dc in range(8):
                        inst = e.matmul(ps[2 * bb + oc][:, :],
                                        xns[:, dc * S + col: dc * S + col + 128],
                                        w[:, dc * D + 512 * oc: dc * D + 512 * (oc + 1)],
                                        start=(dc == 0), stop=(dc == 7))
                inst.then_inc(mm, 1)

        @block.vector
        def _(e):
            cnt = [0]
            rj = 0
            for n, (kind, i) in enumerate(jobs):
                bb = n % 2
                e.wait_ge(mm, n + 1)
                if n >= 2:
                    e.wait_ge(so[n % 2], 16 * (n // 2))
                e.tensor_copy(pt[bb][:, 0:512], ps[2 * bb][:, :])
                last = e.tensor_copy(pt[bb][:, 512:1024], ps[2 * bb + 1][:, :])
                if kind == "v":
                    last.then_inc(ev, 1)
                else:
                    e.wait_ge(lt[rj % 2], 32 * (rj // 2 + 1))
                    fence(e, vch, cnt, last)
                    p4 = pt[bb][:, :].rearrange("p (h two k) -> p h two k", h=H, two=2, k=32)
                    c4 = ct[rj % 2][:, :].rearrange("p (h two k) -> p h two k", h=H, two=2, k=32)
                    s4 = sn[rj % 2][:, :].rearrange("p (h two k) -> p h two k", h=H, two=2, k=32)
                    w4 = tw[bb][:, :].rearrange("p (h two k) -> p h two k", h=H, two=2, k=32)
                    e.tensor_tensor(rt[bb][:, :], pt[bb][:, :], ct[rj % 2][:, :], ALU.mult)
                    e.tensor_tensor(w4[:, :, 0, :], p4[:, :, 1, :], s4[:, :, 0, :], ALU.mult)
                    fence(e, vch, cnt,
                          e.tensor_tensor(w4[:, :, 1, :], p4[:, :, 0, :], s4[:, :, 1, :], ALU.mult))
                    e.tensor_tensor(rt[bb][:, :], rt[bb][:, :], tw[bb][:, :], ALU.add) \
                        .then_inc(ev, 1)
                    rj += 1

        @block.gpsimd
        def _(e):
            for n, (kind, i) in enumerate(jobs):
                bb = n % 2
                e.wait_ge(ev, n + 1)
                dst = {"q": qo, "k": ko, "v": vo}[kind]
                src = rt[bb] if kind != "v" else pt[bb]
                e.dma_start(out=dst[128 * i:128 * (i + 1), :], in_=src[:, :]).then_inc(so[n % 2], 16)
            e.wait_ge(so0, 16 * ((len(jobs) + 1) // 2))
            e.wait_ge(so1, 16 * (len(jobs) // 2))
    return nc



def _l0c_items():
    items = []
    for g in range(4):
        for kb in range(8 * g + 8):
            r = kb - 8 * g
            if r < 0 or r == 0:
                items.append((g, kb, 0, None, True))
            else:
                jmin = r // 2
                tri_j = (r - 1) // 2 if (r % 2 == 1) else None
                items.append((g, kb, jmin, tri_j, False))
    return items


def build_L0C():
    nc = new_nc()
    qh = nc.dram_tensor("qh", [H, 65, TOK], BF16, kind="ExternalInput").ap()
    kh = nc.dram_tensor("kh", [H, 65, S], BF16, kind="ExternalInput").ap()
    vh = nc.dram_tensor("vh", [H, 128, SB * 65], BF16, kind="ExternalInput").ap()
    tri = nc.dram_tensor("tri", [128, 128], BF16, kind="ExternalInput").ap()
    af = nc.dram_tensor("af", [H, 64, TOK], BF16, kind="ExternalOutput").ap()

    Qs = [nc.alloc_sbuf_tensor(f"Qs{i}", [128, TOK], BF16).ap() for i in range(2)]
    Ks = [nc.alloc_sbuf_tensor(f"Ks{i}", [128, S], BF16).ap() for i in range(2)]
    Vs = [nc.alloc_sbuf_tensor(f"Vs{i}", [128, SB * 65], BF16).ap() for i in range(2)]
    p = [nc.alloc_sbuf_tensor(f"p{i}", [128, 512], BF16).ap() for i in range(2)]
    trs = nc.alloc_sbuf_tensor("trs", [128, 128], BF16).ap()
    afh = [nc.alloc_sbuf_tensor(f"afh{i}", [64, TOK], BF16).ap() for i in range(2)]
    araw = nc.alloc_sbuf_tensor("araw", [64, 512], F32).ap()
    rcp = nc.alloc_sbuf_tensor("rcp", [65, 512], BF16).ap()
    scrq = nc.alloc_sbuf_tensor("scrq", [64, TOK], BF16).ap()
    scrk = nc.alloc_sbuf_tensor("scrk", [64, S], BF16).ap()
    mx = nc.alloc_sbuf_tensor("mx", [1, 16], F32).ap()
    mst = nc.alloc_sbuf_tensor("mst", [1, 4], F32).ap()
    mstb = nc.alloc_sbuf_tensor("mstb", [1, 1], BF16).ap()
    nm = nc.alloc_sbuf_tensor("nm", [128, 1], F32).ap()
    onesv = nc.alloc_sbuf_tensor("onesv", [64, 1], BF16).ap()
    onesc = nc.alloc_sbuf_tensor("onesc", [65, 128], BF16).ap()

    ps_s = [nc.alloc_psum_tensor(f"ps_s{i}", [128, 512], F32).ap() for i in range(2)]
    pav = nc.alloc_psum_tensor("pav", [65, 512], F32).ap()
    bc = nc.alloc_psum_tensor("bc", [128, 512], F32).ap()
    stp = [nc.alloc_psum_tensor(f"stp{i}", [1, 512], F32).ap() for i in range(4)]

    items = _l0c_items()
    NI = len(items)
    gend = [li == NI - 1 or items[li + 1][0] != items[li][0] for li in range(NI)]
    tri_cnt = []
    c = 0
    for li, it in enumerate(items):
        if it[3] is not None:
            c += 1
        tri_cnt.append(c)
    NTRI = c

    with (
        nc.semaphore("ld0") as ld0,
        nc.semaphore("ld1") as ld1,
        nc.semaphore("ldm") as ldm,
        nc.semaphore("mmq") as mmq,
        nc.semaphore("ex") as ex,
        nc.semaphore("avq") as avq,
        nc.semaphore("trid") as trid,
        nc.semaphore("nrm") as nrm,
        nc.semaphore("stq") as stq,
        nc.semaphore("stv") as stv,
        nc.semaphore("sch") as sch,
        nc.semaphore("rcd") as rcd,
        nc.semaphore("bcq") as bcq,
        nc.semaphore("vch") as vch,
        nc.semaphore("so0") as so0,
        nc.semaphore("so1") as so1,
        nc.Block() as block,
    ):
        ld = [ld0, ld1]
        so = [so0, so1]

        @block.sync
        def _(e):
            e.dma_start(out=trs[:, :], in_=tri[:, :]).then_inc(ldm, 16)
            for h in range(H):
                if h >= 2:
                    e.wait_ge(nrm, 4 * (h - 1))
                e.dma_start(out=Qs[h % 2][0:65, :], in_=qh[h, :, :]).then_inc(ld[h % 2], 16)
                e.dma_start(out=Ks[h % 2][0:65, :], in_=kh[h, :, :]).then_inc(ld[h % 2], 16)
                e.dma_start(out=Vs[h % 2][:, :], in_=vh[h, :, :]).then_inc(ld[h % 2], 16)

        @block.scalar
        def _(e):
            for h in range(H):
                hb = h % 2
                e.wait_ge(ld[h % 2], 48 * (h // 2 + 1))
                if h >= 1:
                    e.wait_ge(stq, 3 * h)
                e.activation(scrq[:, :], Qs[hb][0:64, :], AF.Square)
                e.activation(scrk[:, :], Ks[hb][0:64, :], AF.Square).then_inc(sch, 1)
                e.wait_ge(stv, 5 * h + 4)
                e.activation(mst[0:1, 3:4], mst[0:1, 2:3], AF.Sqrt).then_inc(sch, 1)
                e.wait_ge(bcq, 5 * h + 1)
                e.activation(nm[:, :], bc[:, 0:1], AF.Identity).then_inc(sch, 1)
                e.wait_ge(sch, 7 * h + 3)
                for li, (g, kb, jmin, tri_j, full) in enumerate(items):
                    i = h * NI + li
                    e.wait_ge(mmq, i + 1)
                    if i >= 2:
                        e.wait_ge(avq, i - 1)
                    e.activation(p[i % 2][:, jmin * 128:512], ps_s[i % 2][:, jmin * 128:512],
                                 AF.Exp, bias=nm[:, 0:1], scale=0.125).then_inc(ex, 1)
                    if gend[li]:
                        e.wait_ge(avq, i + 1)
                        e.activation(araw[:, :], pav[0:64, :], AF.Identity).then_inc(sch, 1)

        @block.tensor
        def _(e):
            e.wait_ge(ldm, 17)

            def emit_av(h, li):
                g, kb, jmin, tri_j, full = items[li]
                i = h * NI + li
                e.wait_ge(ex, i + 1)
                if tri_j is not None:
                    e.wait_ge(trid, h * NTRI + tri_cnt[li])
                first = (kb == 0)
                if first:
                    e.wait_ge(nrm, 4 * h + g)
                e.matmul(pav[:, jmin * 128:512], Vs[h % 2][:, 65 * kb:65 * (kb + 1)],
                         p[i % 2][:, jmin * 128:512],
                         start=first, stop=gend[li]).then_inc(avq, 1)

            for h in range(H):
                hb = h % 2
                e.wait_ge(sch, 7 * h + 1)
                if h >= 1:
                    e.wait_ge(stv, 5 * h - 2)
                for c4 in range(4):
                    inst = e.matmul(stp[c4][:, :], onesv[:, :], scrq[:, 512 * c4:512 * (c4 + 1)],
                                    start=True, stop=True)
                inst.then_inc(stq, 1)
                for rnd in range(2):
                    e.wait_ge(stv, 5 * h + 1 + rnd)
                    for c4 in range(4):
                        inst = e.matmul(stp[c4][:, :], onesv[:, :],
                                        scrk[:, 2048 * rnd + 512 * c4: 2048 * rnd + 512 * (c4 + 1)],
                                        start=True, stop=True)
                    inst.then_inc(stq, 1)
                e.wait_ge(stv, 5 * h + 5)
                e.matmul(bc[:, 0:1], onesc[0:1, :], mstb[0:1, 0:1], start=True, stop=True) \
                    .then_inc(bcq, 1)
                pend = None
                for li, (g, kb, jmin, tri_j, full) in enumerate(items):
                    i = h * NI + li
                    if i >= 2:
                        e.wait_ge(ex, i - 1)
                    if full:
                        inst = e.matmul(ps_s[i % 2][:, :], Ks[hb][0:65, 128 * kb:128 * (kb + 1)],
                                        Qs[hb][0:65, 512 * g:512 * (g + 1)], start=True, stop=True)
                    else:
                        for j in range(jmin, 4):
                            inst = e.matmul(ps_s[i % 2][:, 128 * j:128 * (j + 1)],
                                            Ks[hb][0:65, 128 * kb:128 * (kb + 1)],
                                            Qs[hb][0:65, 512 * g + 128 * j:512 * g + 128 * (j + 1)],
                                            start=True, stop=True)
                    inst.then_inc(mmq, 1)
                    if pend is not None:
                        emit_av(h, pend)
                    pend = li
                    if gend[li]:
                        emit_av(h, pend)
                        pend = None
                        G = 4 * h + g
                        e.wait_ge(rcd, G + 1)
                        e.matmul(bc[0:64, :], onesc[64:65, 0:64], rcp[64:65, :],
                                 start=True, stop=True).then_inc(bcq, 1)

        @block.vector
        def _(e):
            cnt = [0]
            e.memset(onesv[:, :], 1.0)
            e.memset(onesc[0:65, :], 1.0).then_inc(ldm, 1)
            for h in range(H):
                hb = h % 2
                for rnd in range(3):
                    e.wait_ge(stq, 3 * h + rnd + 1)
                    for c4 in range(4):
                        inst = e.tensor_reduce(mx[0:1, 4 * rnd + c4:4 * rnd + c4 + 1],
                                               stp[c4][0:1, :], AX.X, ALU.max)
                    inst.then_inc(stv, 1)
                e.wait_ge(stv, 5 * h + 3)
                fence(e, vch, cnt, e.tensor_reduce(mst[0:1, 0:1], mx[0:1, 0:4], AX.X, ALU.max))
                fence(e, vch, cnt, e.tensor_reduce(mst[0:1, 1:2], mx[0:1, 4:12], AX.X, ALU.max))
                e.tensor_tensor(mst[0:1, 2:3], mst[0:1, 0:1], mst[0:1, 1:2], ALU.mult) \
                    .then_inc(stv, 1)
                e.wait_ge(sch, 7 * h + 2)
                if h >= 1:
                    e.wait_ge(bcq, 5 * (h - 1) + 1)
                e.tensor_scalar(mstb[0:1, 0:1], mst[0:1, 3:4], -0.125, None, ALU.mult) \
                    .then_inc(stv, 1)
                for li, (g, kb, jmin, tri_j, full) in enumerate(items):
                    i = h * NI + li
                    if tri_j is not None:
                        e.wait_ge(ex, i + 1)
                        e.tensor_tensor(p[i % 2][:, 128 * tri_j:128 * (tri_j + 1)],
                                        p[i % 2][:, 128 * tri_j:128 * (tri_j + 1)],
                                        trs[:, :], ALU.mult).then_inc(trid, 1)
                    if gend[li]:
                        G = 4 * h + g
                        e.wait_ge(avq, i + 1)
                        e.wait_ge(sch, 7 * h + 4 + g)
                        with nc.allow_low_precision(reason="bf16 softmax normalizer"):
                            fence(e, vch, cnt, e.reciprocal(rcp[64:65, :], pav[64:65, :]))
                        e.engine_nop().then_inc(rcd, 1)
                        e.wait_ge(bcq, 5 * h + 2 + g)
                        if h >= 2 and g == 0:
                            e.wait_ge(so[h % 2], 16 * (h // 2))
                        e.tensor_tensor(afh[hb][:, 512 * g:512 * (g + 1)], araw[:, :],
                                        bc[0:64, :], ALU.mult).then_inc(nrm, 1)

        @block.gpsimd
        def _(e):
            for h in range(H):
                e.wait_ge(nrm, 4 * (h + 1))
                e.dma_start(out=af[h, :, :], in_=afh[h % 2][:, :]).then_inc(so[h % 2], 16)
            e.wait_ge(so0, 16 * (H // 2))
            e.wait_ge(so1, 16 * (H // 2))
    return nc



def build_L0M():
    nc = new_nc()
    xnf = nc.dram_tensor("xnf", [D, TOK], BF16, kind="ExternalInput").ap()
    xnf32 = nc.dram_tensor("xnf32", [D, TOK], F32, kind="ExternalInput").ap()
    wgT = nc.dram_tensor("wgT", [D, E], F32, kind="ExternalInput").ap()
    bg = nc.dram_tensor("bg", [E, 1], F32, kind="ExternalInput").ap()
    w1T = nc.dram_tensor("w1T", [E, D, DFF], F32, kind="ExternalInput").ap()
    w2T = nc.dram_tensor("w2T", [E, D, D], F32, kind="ExternalInput").ap()
    ident = nc.dram_tensor("ident", [128, 128], F32, kind="ExternalInput").ap()
    moe = nc.dram_tensor("moe", [D, TOK], F32, kind="ExternalOutput").ap()
    cesc = nc.dram_tensor("cesc", [E, TOK], F32, kind="Internal").ap()

    NT = 16
    NT2 = 8
    xns = nc.alloc_sbuf_tensor("xns", [128, 8 * TOK], BF16).ap()
    wgs = nc.alloc_sbuf_tensor("wgs", [128, 8 * E], F32).ap()
    bgs = nc.alloc_sbuf_tensor("bgs", [E, 1], F32).ap()
    ids = nc.alloc_sbuf_tensor("ids", [128, 128], F32).ap()
    onef = nc.alloc_sbuf_tensor("onef", [1, 128], F32).ap()
    lgf = nc.alloc_sbuf_tensor("lgf", [E, TOK], F32).ap()
    lgt = nc.alloc_sbuf_tensor("lgt", [128, NT * E], F32).ap()
    u = nc.alloc_sbuf_tensor("u", [128, NT * E], F32).ap()
    is1 = nc.alloc_sbuf_tensor("is1", [128, NT * E], F32).ap()
    ge_ = nc.alloc_sbuf_tensor("ge_", [128, NT * E], F32).ap()
    stt = nc.alloc_sbuf_tensor("stt", [128, 4 * NT], F32).ap()
    cef = nc.alloc_sbuf_tensor("cef", [E, TOK], F32).ap()
    w1s = [nc.alloc_sbuf_tensor("w1s0", [128, 8 * DFF], BF16).ap()]
    cefe = nc.alloc_sbuf_tensor("cefe", [1, TOK], F32).ap()
    w2s = nc.alloc_sbuf_tensor("w2s", [128, 8 * D], BF16).ap()
    acc = nc.alloc_sbuf_tensor("acc", [128, 8 * TOK], F32).ap()
    a_sb = nc.alloc_sbuf_tensor("a_sb", [128, 8 * 256], BF16).ap()
    t1b = [nc.alloc_sbuf_tensor(f"t1b{i}", [128, 256], BF16).ap() for i in range(2)]
    sg = [nc.alloc_sbuf_tensor(f"sg{i}", [128, 256], BF16).ap() for i in range(2)]
    ces = nc.alloc_sbuf_tensor("ces", [128, 256], BF16).ap()

    pbG = [nc.alloc_psum_tensor(f"pbG{i}", [128, 512], F32).ap() for i in range(2)]
    pbU = [nc.alloc_psum_tensor(f"pbU{i}", [128, 512], F32).ap() for i in range(2)]
    pbC = nc.alloc_psum_tensor("pbC", [128, 512], F32).ap()
    pbW = [nc.alloc_psum_tensor(f"pbW{i}", [128, 512], F32).ap() for i in range(2)]
    pbX = nc.alloc_psum_tensor("pbX", [128, 512], F32).ap()

    V_LG, V_LGT, V_SUB, V_TOP2 = 4, 20, 21, 22
    V_CEF = V_TOP2 + NT
    PER_T2 = 1 + 16 + 8
    PER_EXP = NT2 * PER_T2
    W2EV = {ei: V_CEF + ei * PER_EXP for ei in range(E + 1)}

    def v_ce(ei, t2):
        return W2EV[ei] + t2 * PER_T2 + 1

    def v_t1(ei, t2, cp):
        return v_ce(ei, t2) + 2 * cp + 1

    def v_app(ei, t2, cp):
        return v_ce(ei, t2) + 2 * cp + 2

    def v_acc(ei, t2, oc):
        return v_ce(ei, t2) + 16 + oc + 1

    M_LG, M_TR1, M_TR2 = 4, 20, 36
    MPER_T2 = 25
    M_E = {ei: M_TR2 + ei * NT2 * MPER_T2 for ei in range(E + 1)}

    def m_ceb(ei, t2):
        return M_E[ei] + t2 * MPER_T2 + 1

    def m_gate(ei, t2, cp):
        return M_E[ei] + t2 * MPER_T2 + 2 * cp + 2

    def m_up(ei, t2, cp):
        return M_E[ei] + t2 * MPER_T2 + 2 * cp + 3

    def m_w2(ei, t2, oc):
        return M_E[ei] + t2 * MPER_T2 + 17 + oc + 1

    def s_sil(ei, t2, cp):
        return 1 + (ei * NT2 + t2) * 8 + cp + 1

    silu_flat = [(ei, t2, cp) for ei in range(E) for t2 in range(NT2) for cp in range(8)]

    with (
        nc.semaphore("ld") as ld,
        nc.semaphore("lg0") as lg0,
        nc.semaphore("ldc") as ldc,
        nc.semaphore("lw0") as lw0,
        nc.semaphore("lw1") as lw1,
        nc.semaphore("lw2") as lw2,
        nc.semaphore("mmq") as mmq,
        nc.semaphore("sil") as sil,
        nc.semaphore("vq") as vq,
        nc.semaphore("vch") as vch,
        nc.semaphore("so") as so,
        nc.Block() as block,
    ):
        lw = [lw0, lw1]

        @block.sync
        def _(e):
            for c in range(8):
                e.dma_start(out=xns[:, c * TOK:(c + 1) * TOK],
                            in_=xnf[128 * c:128 * (c + 1), :]).then_inc(ld, 16)
            e.dma_start(out=bgs[:, :], in_=bg[:, :]).then_inc(ld, 16)
            e.dma_start(out=ids[:, :], in_=ident[:, :]).then_inc(ld, 16)
            for c in range(8):
                e.dma_start(out=acc[:, c * TOK:(c + 1) * TOK],
                            in_=xnf32[128 * c:128 * (c + 1), :]).then_inc(ld, 16)
            for c in range(8):
                e.dma_start(out=wgs[:, 8 * c:8 * (c + 1)],
                            in_=wgT[128 * c:128 * (c + 1), :]).then_inc(ld, 16)
            e.wait_ge(vq, V_CEF)
            e.dma_start(out=cesc[:, :], in_=cef[0:8, :]).then_inc(ldc, 16)
            for ei in range(E):
                e.wait_ge(ldc, 16 + 16 * ei)
                if ei >= 1:
                    e.wait_ge(mmq, m_ceb(ei - 1, NT2 - 1))
                e.dma_start(out=cefe[0:1, :], in_=cesc[ei:ei + 1, :]).then_inc(ldc, 16)

        @block.gpsimd
        def _(e):
            for ei in range(E):
                if ei >= 1:
                    e.wait_ge(mmq, m_up(ei - 1, NT2 - 1, 7))
                for c in range(8):
                    e.dma_start(out=w1s[0][:, c * DFF:(c + 1) * DFF],
                                in_=w1T[ei, 128 * c:128 * (c + 1), :]).then_inc(lw[ei % 2], 16)
                if ei >= 1:
                    e.wait_ge(vq, W2EV[ei])
                for c in range(8):
                    e.dma_start(out=w2s[:, c * D:(c + 1) * D],
                                in_=w2T[ei, 128 * c:128 * (c + 1), :]).then_inc(lw2, 16)
            for c in range(8):
                e.wait_ge(vq, W2EV[E])
                e.dma_start(out=moe[128 * c:128 * (c + 1), :],
                            in_=acc[:, c * TOK:(c + 1) * TOK]).then_inc(so, 16)
            e.wait_ge(so, 16 * 8)

        @block.tensor
        def _(e):
            e.wait_ge(ld, 16 * 26)
            for tc4 in range(4):
                if tc4 >= 1:
                    e.wait_ge(vq, tc4)
                for dc in range(8):
                    inst = e.matmul(pbX[0:8, :], wgs[:, 8 * dc:8 * (dc + 1)],
                                    acc[:, dc * TOK + 512 * tc4: dc * TOK + 512 * (tc4 + 1)],
                                    start=(dc == 0), stop=(dc == 7))
                inst.then_inc(mmq, 1)
            for tc in range(NT):
                e.wait_ge(vq, V_LG if tc == 0 else V_LG + tc)
                e.transpose(pbX[:, 0:8], lgf[0:8, 128 * tc:128 * (tc + 1)], ids[0:8, 0:8]) \
                    .then_inc(mmq, 1)
            for tc in range(NT):
                e.wait_ge(vq, V_TOP2 if tc == 0 else V_TOP2 + tc)
                e.transpose(pbX[0:8, 0:128], u[:, 8 * tc:8 * (tc + 1)], ids[:, :]) \
                    .then_inc(mmq, 1)
            for ei in range(E):
                e.wait_ge(lw[ei % 2], 16 * 8 * (ei // 2 + 1))
                for t2 in range(NT2):
                    col = 256 * t2
                    prev = (ei, t2 - 1) if t2 > 0 else ((ei - 1, NT2 - 1) if ei > 0 else None)
                    if prev is not None:
                        e.wait_ge(vq, v_ce(*prev))
                    e.wait_ge(ldc, 16 + 16 * (ei + 1))
                    e.matmul(pbC[:, 0:256], onef[0:1, :], cefe[0:1, col:col + 256],
                             start=True, stop=True).then_inc(mmq, 1)
                    for cp in range(8):
                        bb = cp % 2
                        if cp >= 2:
                            e.wait_ge(sil, s_sil(ei, t2, cp - 2))
                            e.wait_ge(vq, v_t1(ei, t2, cp - 2))
                        elif t2 > 0 and cp < 2:
                            e.wait_ge(sil, s_sil(ei, t2 - 1, cp + 6))
                            e.wait_ge(vq, v_t1(ei, t2 - 1, cp + 6))
                        elif t2 == 0 and ei > 0 and cp < 2:
                            e.wait_ge(sil, s_sil(ei - 1, NT2 - 1, cp + 6))
                            e.wait_ge(vq, v_t1(ei - 1, NT2 - 1, cp + 6))
                        for dc in range(8):
                            inst = e.matmul(pbG[bb][:, 0:256],
                                            w1s[0][:, dc * DFF + 128 * cp: dc * DFF + 128 * cp + 128],
                                            xns[:, dc * TOK + col: dc * TOK + col + 256],
                                            start=(dc == 0), stop=(dc == 7))
                        inst.then_inc(mmq, 1)
                        for dc in range(8):
                            inst = e.matmul(pbU[bb][:, 0:256],
                                            w1s[0][:, dc * DFF + 1024 + 128 * cp: dc * DFF + 1024 + 128 * cp + 128],
                                            xns[:, dc * TOK + col: dc * TOK + col + 256],
                                            start=(dc == 0), stop=(dc == 7))
                        inst.then_inc(mmq, 1)
                    e.wait_ge(vq, v_app(ei, t2, 7))
                    e.wait_ge(lw2, 16 * 8 * (ei + 1))
                    for oc in range(8):
                        bb = oc % 2
                        if oc >= 2:
                            e.wait_ge(vq, v_acc(ei, t2, oc - 2))
                        elif t2 > 0:
                            e.wait_ge(vq, v_acc(ei, t2 - 1, oc + 6))
                        elif ei > 0:
                            e.wait_ge(vq, v_acc(ei - 1, NT2 - 1, oc + 6))
                        for cp in range(8):
                            inst = e.matmul(pbW[bb][:, 0:256],
                                            w2s[:, cp * D + 128 * oc: cp * D + 128 * oc + 128],
                                            a_sb[:, 256 * cp:256 * (cp + 1)],
                                            start=(cp == 0), stop=(cp == 7))
                        inst.then_inc(mmq, 1)

        @block.scalar
        def _(e):
            e.wait_ge(vq, V_SUB)
            e.activation(u[:, :], u[:, :], AF.Exp).then_inc(sil, 1)
            for n, (ei, t2, cp) in enumerate(silu_flat):
                bb = cp % 2
                e.wait_ge(mmq, m_gate(ei, t2, cp))
                if n >= 2:
                    e.wait_ge(vq, v_t1(*silu_flat[n - 2]))
                e.activation(sg[bb][:, :], pbG[bb][:, 0:256], AF.Silu).then_inc(sil, 1)

        @block.vector
        def _(e):
            cnt = [0]
            e.memset(onef[:, :], 1.0)
            for tc4 in range(4):
                e.wait_ge(mmq, tc4 + 1)
                e.tensor_scalar(lgf[0:8, 512 * tc4:512 * (tc4 + 1)], pbX[0:8, :],
                                bgs[:, 0:1], None, ALU.add).then_inc(vq, 1)
            for tc in range(NT):
                e.wait_ge(mmq, M_LG + tc + 1)
                e.tensor_copy(lgt[:, 8 * tc:8 * (tc + 1)], pbX[:, 0:8]).then_inc(vq, 1)
            e.wait_ge(vq, V_LGT)
            lgt3 = lgt[:, :].rearrange("p (t k) -> p t k", t=NT, k=E)
            fence(e, vch, cnt, e.tensor_reduce(stt[:, 0:NT], lgt3, AX.X, ALU.max))
            for tc in range(NT):
                inst = e.tensor_scalar(u[:, 8 * tc:8 * (tc + 1)], lgt[:, 8 * tc:8 * (tc + 1)],
                                       stt[:, tc:tc + 1], None, ALU.subtract)
            inst.then_inc(vq, 1)
            e.wait_ge(sil, 1)
            u3 = u[:, :].rearrange("p (t k) -> p t k", t=NT, k=E)
            fence(e, vch, cnt, e.tensor_reduce(stt[:, NT:2 * NT], u3, AX.X, ALU.max))
            for tc in range(NT):
                inst = e.tensor_scalar(is1[:, 8 * tc:8 * (tc + 1)], u[:, 8 * tc:8 * (tc + 1)],
                                       stt[:, NT + tc:NT + tc + 1], None, ALU.is_equal)
            fence(e, vch, cnt, inst)
            for tc in range(NT):
                inst = e.scalar_tensor_tensor(ge_[:, 8 * tc:8 * (tc + 1)],
                                              is1[:, 8 * tc:8 * (tc + 1)], -2.0,
                                              u[:, 8 * tc:8 * (tc + 1)], ALU.mult, ALU.add)
            fence(e, vch, cnt, inst)
            ge3 = ge_[:, :].rearrange("p (t k) -> p t k", t=NT, k=E)
            fence(e, vch, cnt, e.tensor_reduce(stt[:, 2 * NT:3 * NT], ge3, AX.X, ALU.max))
            for tc in range(NT):
                inst = e.tensor_scalar(is1[:, 8 * tc:8 * (tc + 1)], u[:, 8 * tc:8 * (tc + 1)],
                                       stt[:, 2 * NT + tc:2 * NT + tc + 1], None, ALU.is_ge)
            fence(e, vch, cnt, inst)
            fence(e, vch, cnt, e.tensor_tensor(u[:, :], u[:, :], is1[:, :], ALU.mult))
            u3b = u[:, :].rearrange("p (t k) -> p t k", t=NT, k=E)
            fence(e, vch, cnt, e.tensor_reduce(stt[:, 3 * NT:4 * NT], u3b, AX.X, ALU.add))
            fence(e, vch, cnt, e.reciprocal(stt[:, 3 * NT:4 * NT], stt[:, 3 * NT:4 * NT]))
            for tc in range(NT):
                inst = e.tensor_scalar(u[:, 8 * tc:8 * (tc + 1)], u[:, 8 * tc:8 * (tc + 1)],
                                       stt[:, 3 * NT + tc:3 * NT + tc + 1], None, ALU.mult)
            inst.then_inc(vq, 1)
            for tc in range(NT):
                e.wait_ge(mmq, M_TR1 + tc + 1)
                e.tensor_copy(cef[0:8, 128 * tc:128 * (tc + 1)], pbX[0:8, 0:128]).then_inc(vq, 1)
            for ei in range(E):
                for t2 in range(NT2):
                    col = 256 * t2
                    e.wait_ge(mmq, m_ceb(ei, t2))
                    inst = e.tensor_copy(ces[:, :], pbC[:, 0:256]).then_inc(vq, 1)
                    e.wait_ge(vq, v_ce(ei, t2))
                    for cp in range(8):
                        bb = cp % 2
                        e.wait_ge(sil, s_sil(ei, t2, cp))
                        e.wait_ge(mmq, m_up(ei, t2, cp))
                        e.tensor_tensor(t1b[bb][:, :], sg[bb][:, :], pbU[bb][:, 0:256],
                                        ALU.mult).then_inc(vq, 1)
                        e.wait_ge(vq, v_t1(ei, t2, cp))
                        e.tensor_tensor(a_sb[:, 256 * cp:256 * (cp + 1)], t1b[bb][:, :],
                                        ces[:, :], ALU.mult).then_inc(vq, 1)
                    for oc in range(8):
                        bb = oc % 2
                        e.wait_ge(mmq, m_w2(ei, t2, oc))
                        dst = acc[:, oc * TOK + col: oc * TOK + col + 256]
                        if ei == 0:
                            e.tensor_copy(dst, pbW[bb][:, 0:256]).then_inc(vq, 1)
                        else:
                            e.tensor_tensor(dst, dst, pbW[bb][:, 0:256], ALU.add).then_inc(vq, 1)
    return nc



def build_L0D():
    nc = new_nc()
    attnf = nc.dram_tensor("attnf", [D, TOK], BF16, kind="ExternalInput").ap()
    woT = nc.dram_tensor("woT", [D, D], F32, kind="ExternalInput").ap()
    xo_ = nc.dram_tensor("xo", [TOK, D], F32, kind="ExternalInput").ap()
    moet = nc.dram_tensor("moet", [TOK, D], F32, kind="ExternalInput").ap()
    x1 = nc.dram_tensor("x1", [TOK, D], F32, kind="ExternalOutput").ap()

    afs = nc.alloc_sbuf_tensor("afs", [128, 8 * TOK], BF16).ap()
    wos = nc.alloc_sbuf_tensor("wos", [128, 8 * D], BF16).ap()
    xs = [nc.alloc_sbuf_tensor(f"xs{i}", [128, D], F32).ap() for i in range(2)]
    ms = [nc.alloc_sbuf_tensor(f"ms{i}", [128, D], F32).ap() for i in range(2)]
    x1s = [nc.alloc_sbuf_tensor(f"x1s{i}", [128, D], F32).ap() for i in range(2)]
    ps = [nc.alloc_psum_tensor(f"ps{i}", [128, 512], F32).ap() for i in range(4)]

    NT = 16
    with (
        nc.semaphore("ld") as ld,
        nc.semaphore("lx0") as lx0,
        nc.semaphore("lx1") as lx1,
        nc.semaphore("lw") as lwm,
        nc.semaphore("mmq") as mmq,
        nc.semaphore("vq") as vq,
        nc.semaphore("vch") as vch,
        nc.semaphore("so0") as so0,
        nc.semaphore("so1") as so1,
        nc.Block() as block,
    ):
        lx = [lx0, lx1]
        so = [so0, so1]

        @block.sync
        def _(e):
            for c in range(8):
                e.dma_start(out=afs[:, c * TOK:(c + 1) * TOK],
                            in_=attnf[128 * c:128 * (c + 1), :]).then_inc(ld, 16)
            for tc in range(NT):
                if tc >= 2:
                    e.wait_ge(vq, tc - 1)
                e.dma_start(out=xs[tc % 2][:, :], in_=xo_[128 * tc:128 * (tc + 1), :]) \
                    .then_inc(lx[tc % 2], 16)
                e.dma_start(out=ms[tc % 2][:, :], in_=moet[128 * tc:128 * (tc + 1), :]) \
                    .then_inc(lx[tc % 2], 16)

        @block.gpsimd
        def _(e):
            for c in range(8):
                e.dma_start(out=wos[:, c * D:(c + 1) * D],
                            in_=woT[128 * c:128 * (c + 1), :]).then_inc(lwm, 16)
            for tc in range(NT):
                e.wait_ge(vq, tc + 1)
                e.dma_start(out=x1[128 * tc:128 * (tc + 1), :], in_=x1s[tc % 2][:, :]) \
                    .then_inc(so[tc % 2], 16)
            e.wait_ge(so0, 16 * (NT // 2))
            e.wait_ge(so1, 16 * (NT // 2))

        @block.tensor
        def _(e):
            e.wait_ge(ld, 16 * 8)
            e.wait_ge(lwm, 16 * 8)
            for tc in range(NT):
                bb = tc % 2
                if tc >= 2:
                    e.wait_ge(vq, tc - 1)
                for oc in range(2):
                    for dc in range(8):
                        inst = e.matmul(ps[2 * bb + oc][:, :],
                                        afs[:, dc * TOK + 128 * tc: dc * TOK + 128 * (tc + 1)],
                                        wos[:, dc * D + 512 * oc: dc * D + 512 * (oc + 1)],
                                        start=(dc == 0), stop=(dc == 7))
                inst.then_inc(mmq, 1)

        @block.vector
        def _(e):
            cnt = [0]
            for tc in range(NT):
                bb = tc % 2
                e.wait_ge(mmq, tc + 1)
                e.wait_ge(lx[tc % 2], 32 * (tc // 2 + 1))
                if tc >= 2:
                    e.wait_ge(so[tc % 2], 16 * (tc // 2))
                for oc in range(2):
                    inst = e.tensor_tensor(x1s[bb][:, 512 * oc:512 * (oc + 1)],
                                           xs[bb][:, 512 * oc:512 * (oc + 1)],
                                           ps[2 * bb + oc][:, :], ALU.add)
                fence(e, vch, cnt, inst)
                for oc in range(2):
                    inst = e.tensor_tensor(x1s[bb][:, 512 * oc:512 * (oc + 1)],
                                           x1s[bb][:, 512 * oc:512 * (oc + 1)],
                                           ms[bb][:, 512 * oc:512 * (oc + 1)], ALU.add)
                inst.then_inc(vq, 1)
    return nc



def build_L1A():
    nc = new_nc()
    x1 = nc.dram_tensor("x1", [TOK, D], F32, kind="ExternalInput").ap()
    ident = nc.dram_tensor("ident", [128, 128], F32, kind="ExternalInput").ap()
    xn1 = nc.dram_tensor("xn1", [TOK, D], BF16, kind="ExternalOutput").ap()
    xq = nc.dram_tensor("xq", [TOK, D], BF16, kind="ExternalOutput").ap()
    sxi = nc.dram_tensor("sxi", [16, 128], F32, kind="ExternalOutput").ap()

    NJ = 16
    xs = nc.alloc_sbuf_tensor("xs", [128, NJ * D], F32).ap()
    xo = nc.alloc_sbuf_tensor("xo", [128, NJ * D], BF16).ap()
    xqo = nc.alloc_sbuf_tensor("xqo", [128, NJ * D], BF16).ap()
    sc = nc.alloc_sbuf_tensor("sc", [128, NJ * D], BF16).ap()
    st = nc.alloc_sbuf_tensor("st", [128, 112], F32).ap()
    qt = nc.alloc_sbuf_tensor("qt", [128, D], F32).ap()
    ids = nc.alloc_sbuf_tensor("ids", [128, 128], F32).ap()
    sx16 = nc.alloc_sbuf_tensor("sx16", [16, 128], F32).ap()
    pst = nc.alloc_psum_tensor("pst", [16, 128], F32).ap()

    with (
        nc.semaphore("ld") as ld,
        nc.semaphore("ldi") as ldi,
        nc.semaphore("sA") as sA,
        nc.semaphore("sV") as sV,
        nc.semaphore("mq") as mq,
        nc.semaphore("vch") as vch,
        nc.semaphore("so") as so,
        nc.Block() as block,
    ):
        @block.sync
        def _(e):
            e.dma_start(out=ids[:, :], in_=ident[:, :]).then_inc(ldi, 16)
            for j in range(NJ):
                e.dma_start(out=xs[:, j * D:(j + 1) * D],
                            in_=x1[128 * j:128 * (j + 1), :]).then_inc(ld, 16)

        @block.tensor
        def _(e):
            e.wait_ge(ldi, 16)
            e.wait_ge(sV, 4)
            e.transpose(pst[:, :], st[:, 96:112], ids[:, :]).then_inc(mq, 1)

        @block.scalar
        def _(e):
            e.wait_ge(ld, 16 * NJ)
            last = None
            for j in range(NJ):
                last = e.activation(sc[:, j * D:(j + 1) * D], xs[:, j * D:(j + 1) * D],
                                    AF.Identity, accum_out=st[:, j:j + 1])
            last.then_inc(sA, 1)
            e.wait_ge(sV, 1)
            for j in range(NJ):
                last = e.activation(sc[:, j * D:(j + 1) * D], xs[:, j * D:(j + 1) * D],
                                    AF.Square, accum_out=st[:, 16 + j:17 + j])
            last.then_inc(sA, 1)
            e.wait_ge(sV, 2)
            e.activation(st[:, 0:16], st[:, 32:48], AF.Sqrt).then_inc(sA, 1)

        @block.vector
        def _(e):
            cnt = [0]
            e.wait_ge(sA, 1)
            fence(e, vch, cnt,
                  e.tensor_scalar(st[:, 48:64], st[:, 0:16], 1.0 / D, None, ALU.mult))
            last = None
            for j in range(NJ):
                last = e.tensor_scalar(xs[:, j * D:(j + 1) * D], xs[:, j * D:(j + 1) * D],
                                       st[:, 48 + j:49 + j], None, ALU.subtract)
            last.then_inc(sV, 1)
            e.wait_ge(sA, 2)
            e.tensor_scalar(st[:, 32:48], st[:, 16:32], 1.0 / D, 1e-5, ALU.mult, ALU.add) \
                .then_inc(sV, 1)
            e.wait_ge(sA, 3)
            fence(e, vch, cnt, e.reciprocal(st[:, 32:48], st[:, 0:16]))
            for j in range(NJ):
                last = e.tensor_scalar(xs[:, j * D:(j + 1) * D], xs[:, j * D:(j + 1) * D],
                                       st[:, 32 + j:33 + j], None, ALU.mult)
            fence(e, vch, cnt, last)
            for j in range(NJ):
                last = e.tensor_copy(xo[:, j * D:(j + 1) * D], xs[:, j * D:(j + 1) * D])
            last.then_inc(sV, 1)
            for j in range(NJ):
                last = e.tensor_reduce(st[:, 64 + j:65 + j], xs[:, j * D:(j + 1) * D],
                                       AX.X, ALU.max, apply_absolute_value=True)
            fence(e, vch, cnt, last)
            fence(e, vch, cnt,
                  e.tensor_scalar_max(st[:, 64:80], st[:, 64:80], 1e-5))
            fence(e, vch, cnt, e.reciprocal(st[:, 80:96], st[:, 64:80]))
            fence(e, vch, cnt,
                  e.tensor_scalar(st[:, 80:96], st[:, 80:96], 127.0, None, ALU.mult))
            e.tensor_scalar(st[:, 96:112], st[:, 64:80], 1.0 / 127.0, None, ALU.mult) \
                .then_inc(sV, 1)
            e.wait_ge(mq, 1)
            e.tensor_copy(sx16[:, :], pst[:, :]).then_inc(sV, 1)
            for j in range(NJ):
                e.tensor_scalar(qt[:, :], xs[:, j * D:(j + 1) * D],
                                st[:, 80 + j:81 + j], MAGIC, ALU.mult, ALU.add)
                e.wait_ge(vch, cnt[0] + 1) if False else None
                fence(e, vch, cnt, e.engine_nop())
                e.tensor_scalar(qt[:, :], qt[:, :], -MAGIC, 127.0, ALU.add, ALU.min)
                fence(e, vch, cnt, e.engine_nop())
                last = e.tensor_scalar(xqo[:, j * D:(j + 1) * D], qt[:, :], -128.0, None,
                                       ALU.max)
                fence(e, vch, cnt, last)
            e.engine_nop().then_inc(sV, 1)

        @block.gpsimd
        def _(e):
            e.wait_ge(sV, 3)
            for j in range(NJ):
                e.dma_start(out=xn1[128 * j:128 * (j + 1), :],
                            in_=xo[:, j * D:(j + 1) * D]).then_inc(so, 16)
            e.wait_ge(sV, 5)
            e.dma_start(out=sxi[:, :], in_=sx16[:, :]).then_inc(so, 16)
            e.wait_ge(sV, 6)
            for j in range(NJ):
                e.dma_start(out=xq[128 * j:128 * (j + 1), :],
                            in_=xqo[:, j * D:(j + 1) * D]).then_inc(so, 16)
            e.wait_ge(so, 16 * (2 * NJ + 1))
    return nc



def build_L1B():
    nc = new_nc()
    xn1f = nc.dram_tensor("xn1f", [D, TOK], BF16, kind="ExternalInput").ap()
    wqkvT = nc.dram_tensor("wqkvT", [D, 3 * D], F32, kind="ExternalInput").ap()
    qf = nc.dram_tensor("qf", [D, TOK], BF16, kind="ExternalOutput").ap()
    kvp = nc.dram_tensor("kvp", [128, 8 * 65], F32, kind="ExternalOutput").ap()

    NT = 16
    xns = nc.alloc_sbuf_tensor("xns", [128, 8 * TOK], BF16).ap()
    ws = nc.alloc_sbuf_tensor("ws", [128, 8 * 3 * D], BF16).ap()
    qfs = nc.alloc_sbuf_tensor("qfs", [128, 8 * TOK], BF16).ap()
    kts = nc.alloc_sbuf_tensor("kts", [128, D], BF16).ap()
    vts = nc.alloc_sbuf_tensor("vts", [128, 16 * 65], BF16).ap()
    t1 = [nc.alloc_sbuf_tensor(f"t1_{i}", [128, 512], F32).ap() for i in range(2)]
    t2 = [nc.alloc_sbuf_tensor(f"t2_{i}", [128, 512], F32).ap() for i in range(2)]
    te = [nc.alloc_sbuf_tensor(f"te_{i}", [128, 512], F32).ap() for i in range(2)]
    kvs = nc.alloc_sbuf_tensor("kvs", [128, 8 * 65], F32).ap()

    psq = [nc.alloc_psum_tensor(f"psq{i}", [128, 512], F32).ap() for i in range(2)]
    psk = [nc.alloc_psum_tensor(f"psk{i}", [128, 512], F32).ap() for i in range(2)]
    psv = [nc.alloc_psum_tensor(f"psv{i}", [128, 512], F32).ap() for i in range(2)]
    kvA_t = nc.alloc_psum_tensor("kvA", [128, 512], F32).ap()
    kvB_t = nc.alloc_psum_tensor("kvB", [128, 512], F32).ap()
    kvA = kvA_t[:, 0:260]
    kvB = kvB_t[:, 0:260]

    def m_q(j):
        return j + 1

    def m_k(tc, i):
        return 32 + 5 * tc + i + 1

    def m_v(tc, i):
        return 32 + 5 * tc + 2 + i + 1

    def m_kv(tc):
        return 32 + 5 * tc + 5

    def v_qmin(j):
        return 2 * j + 1

    def v_qdone(j):
        return 2 * j + 2

    def v_kmin(tc, i):
        return 64 + 5 * tc + 2 * i + 1

    def v_kdone(tc, i):
        return 64 + 5 * tc + 2 * i + 2

    def v_vaug(tc):
        return 64 + 5 * tc + 5

    V_KV = 64 + 5 * NT + 1

    elu_jobs = [("q", j) for j in range(32)] + [("k", tc, i) for tc in range(NT) for i in range(2)]

    def a_exp(n):
        return n + 1

    with (
        nc.semaphore("ld") as ld,
        nc.semaphore("lw") as lwm,
        nc.semaphore("mmq") as mmq,
        nc.semaphore("vq") as vq,
        nc.semaphore("aq") as aq,
        nc.semaphore("vch") as vch,
        nc.semaphore("kvq") as kvq,
        nc.semaphore("msk") as msk,
        nc.semaphore("so") as so,
        nc.Block() as block,
    ):
        @block.sync
        def _(e):
            for c in range(8):
                e.dma_start(out=xns[:, c * TOK:(c + 1) * TOK],
                            in_=xn1f[128 * c:128 * (c + 1), :]).then_inc(ld, 16)

        @block.gpsimd
        def _(e):
            for c in range(8):
                e.dma_start(out=ws[:, c * 3 * D:(c + 1) * 3 * D],
                            in_=wqkvT[128 * c:128 * (c + 1), :]).then_inc(lwm, 16)
            for oc in range(8):
                e.wait_ge(vq, v_qdone(4 * oc + 3))
                e.dma_start(out=qf[128 * oc:128 * (oc + 1), :],
                            in_=qfs[:, oc * TOK:(oc + 1) * TOK]).then_inc(so, 16)
            e.wait_ge(kvq, NT)
            e.dma_start(out=kvp[:, :], in_=kvs[:, :]).then_inc(so, 16)
            e.wait_ge(so, 16 * 9)

        @block.tensor
        def _(e):
            e.wait_ge(ld, 16 * 8)
            e.wait_ge(lwm, 16 * 8)
            for j in range(32):
                oc, tg = j // 4, j % 4
                bb = j % 2
                if j >= 2:
                    e.wait_ge(vq, v_qdone(j - 2))
                for dc in range(8):
                    inst = e.matmul(psq[bb][:, :],
                                    ws[:, dc * 3 * D + 128 * oc: dc * 3 * D + 128 * oc + 128],
                                    xns[:, dc * TOK + 512 * tg: dc * TOK + 512 * (tg + 1)],
                                    start=(dc == 0), stop=(dc == 7))
                inst.then_inc(mmq, 1)
            for tc in range(NT):
                col = 128 * tc
                for i in range(2):
                    if tc >= 1:
                        e.wait_ge(vq, v_kdone(tc - 1, i))
                    for dc in range(8):
                        inst = e.matmul(psk[i][:, :],
                                        xns[:, dc * TOK + col: dc * TOK + col + 128],
                                        ws[:, dc * 3 * D + D + 512 * i: dc * 3 * D + D + 512 * (i + 1)],
                                        start=(dc == 0), stop=(dc == 7))
                    inst.then_inc(mmq, 1)
                for i in range(2):
                    if tc >= 1:
                        e.wait_ge(vq, v_vaug(tc - 1))
                    for dc in range(8):
                        inst = e.matmul(psv[i][:, :],
                                        xns[:, dc * TOK + col: dc * TOK + col + 128],
                                        ws[:, dc * 3 * D + 2 * D + 512 * i: dc * 3 * D + 2 * D + 512 * (i + 1)],
                                        start=(dc == 0), stop=(dc == 7))
                    inst.then_inc(mmq, 1)
                e.wait_ge(vq, v_vaug(tc))
                if tc >= 1:
                    e.wait_ge(kvq, tc)
                for h in range(H):
                    hp = h // 2
                    pb = kvA_t if hp < 4 else kvB_t
                    fo = (hp % 4) * 65
                    po = (h % 2) * 64
                    inst = e.matmul(pb[po:po + 64, fo:fo + 65],
                                    kts[:, 64 * h:64 * (h + 1)],
                                    vts[:, 65 * h:65 * (h + 1)],
                                    start=True, stop=True)
                inst.then_inc(mmq, 1)

        @block.scalar
        def _(e):
            for n, job in enumerate(elu_jobs):
                bb = n % 2
                if job[0] == "q":
                    e.wait_ge(vq, v_qmin(job[1]))
                else:
                    e.wait_ge(vq, v_kmin(job[1], job[2]))
                if n >= 2:
                    prev = elu_jobs[n - 2]
                    e.wait_ge(vq, v_qdone(prev[1]) if prev[0] == "q"
                              else v_kdone(prev[1], prev[2]))
                e.activation(te[bb][:, :], t1[bb][:, :], AF.Exp).then_inc(aq, 1)

        @block.vector
        def _(e):
            cnt = [0]
            e.memset(kvs[:, :], 0.0).then_inc(msk, 1)
            n = 0
            for j in range(32):
                bb = n % 2
                e.wait_ge(mmq, m_q(j))
                if n >= 2:
                    e.wait_ge(aq, n - 1)
                e.tensor_scalar_min(t1[bb][:, :], psq[j % 2][:, :], 0.0).then_inc(vq, 1)
                fence(e, vch, cnt, e.tensor_scalar_max(t2[bb][:, :], psq[j % 2][:, :], 0.0))
                e.wait_ge(aq, n + 1)
                oc, tg = j // 4, j % 4
                e.tensor_tensor(qfs[:, oc * TOK + 512 * tg: oc * TOK + 512 * (tg + 1)],
                                t2[bb][:, :], te[bb][:, :], ALU.add).then_inc(vq, 1)
                n += 1
            for tc in range(NT):
                for i in range(2):
                    bb = n % 2
                    e.wait_ge(mmq, m_k(tc, i))
                    if n >= 2:
                        e.wait_ge(aq, n - 1)
                    e.tensor_scalar_min(t1[bb][:, :], psk[i][:, :], 0.0).then_inc(vq, 1)
                    fence(e, vch, cnt, e.tensor_scalar_max(t2[bb][:, :], psk[i][:, :], 0.0))
                    e.wait_ge(aq, n + 1)
                    e.tensor_tensor(kts[:, 512 * i:512 * (i + 1)],
                                    t2[bb][:, :], te[bb][:, :], ALU.add).then_inc(vq, 1)
                    n += 1
                e.wait_ge(mmq, m_v(tc, 1))
                v16 = vts[:, :].rearrange("p (h c) -> p h c", h=H, c=65)
                e.memset(vts[:, :].rearrange("p (h c) -> p h c", h=H, c=65)[:, :, 64:65], 1.0)
                p0 = psv[0][:, :].rearrange("p (h c) -> p h c", h=8, c=64)
                p1 = psv[1][:, :].rearrange("p (h c) -> p h c", h=8, c=64)
                e.tensor_copy(v16[:, 0:8, 0:64], p0)
                e.tensor_copy(v16[:, 8:16, 0:64], p1).then_inc(vq, 1)
                e.wait_ge(mmq, m_kv(tc))
                if tc == 0:
                    e.wait_ge(msk, 1)
                e.tensor_tensor(kvs[:, 0:260], kvs[:, 0:260], kvA, ALU.add)
                e.tensor_tensor(kvs[:, 260:520], kvs[:, 260:520], kvB, ALU.add) \
                    .then_inc(kvq, 1)
    return nc



def build_L1C():
    nc = new_nc()
    kv0 = nc.dram_tensor("kv0", [128, 8 * 65], F32, kind="ExternalInput").ap()
    kv1 = nc.dram_tensor("kv1", [128, 8 * 65], F32, kind="ExternalInput").ap()
    qfi = nc.dram_tensor("qfi", [D, TOK], BF16, kind="ExternalInput").ap()
    af1 = nc.dram_tensor("af1", [H, 64, TOK], BF16, kind="ExternalOutput").ap()

    kva = nc.alloc_sbuf_tensor("kva", [128, 520], F32).ap()
    kvb_ = nc.alloc_sbuf_tensor("kvb_", [128, 520], F32).ap()
    kvb16 = nc.alloc_sbuf_tensor("kvb16", [128, 520], BF16).ap()
    qfs = nc.alloc_sbuf_tensor("qfs", [128, 8 * TOK], BF16).ap()
    ons = nc.alloc_sbuf_tensor("ons", [128, 64], BF16).ap()
    nrw = nc.alloc_sbuf_tensor("nrw", [128, 512], F32).ap()
    rcw = nc.alloc_sbuf_tensor("rcw", [128, 512], BF16).ap()
    araw = [nc.alloc_sbuf_tensor(f"araw{i}", [64, 512], F32).ap() for i in range(2)]
    afh = [nc.alloc_sbuf_tensor(f"afh{i}", [64, TOK], BF16).ap() for i in range(2)]

    pso = [nc.alloc_psum_tensor(f"pso{i}", [65, 512], F32).ap() for i in range(2)]
    pbc = [nc.alloc_psum_tensor(f"pbc{i}", [64, 512], F32).ap() for i in range(2)]

    with (
        nc.semaphore("ld") as ld,
        nc.semaphore("ldk") as ldk,
        nc.semaphore("mmq") as mmq,
        nc.semaphore("vq") as vq,
        nc.semaphore("aq") as aq,
        nc.semaphore("vch") as vch,
        nc.semaphore("so0") as so0,
        nc.semaphore("so1") as so1,
        nc.Block() as block,
    ):
        so = [so0, so1]

        @block.sync
        def _(e):
            e.dma_start(out=kva[:, :], in_=kv0[:, :]).then_inc(ldk, 16)
            e.dma_start(out=kvb_[:, :], in_=kv1[:, :]).then_inc(ldk, 16)
            for c in range(8):
                e.dma_start(out=qfs[:, c * TOK:(c + 1) * TOK],
                            in_=qfi[128 * c:128 * (c + 1), :]).then_inc(ld, 16)

        @block.vector
        def _(e):
            cnt = [0]
            e.memset(ons[:, :], 1.0)
            e.wait_ge(ldk, 32)
            fence(e, vch, cnt, e.tensor_add(kva[:, :], kva[:, :], kvb_[:, :]))
            e.tensor_copy(kvb16[:, :], kva[:, :]).then_inc(vq, 1)
            for h in range(H):
                for g in range(4):
                    un = 4 * h + g
                    bb = un % 2
                    e.wait_ge(mmq, 2 * un + 1)
                    fence(e, vch, cnt,
                          e.tensor_scalar_max(nrw[64:65, :], pso[bb][64:65, :], 1e-6))
                    with nc.allow_low_precision(reason="bf16 linear-attn normalizer"):
                        fence(e, vch, cnt, e.reciprocal(rcw[64:65, :], nrw[64:65, :]))
                    e.engine_nop().then_inc(vq, 1)
                    e.wait_ge(mmq, 2 * un + 2)
                    e.wait_ge(aq, un + 1)
                    if un >= 4 and g == 0:
                        e.wait_ge(so[h % 2], 16 * (h // 2))
                    e.tensor_tensor(afh[h % 2][:, 512 * g:512 * (g + 1)], araw[bb][:, :],
                                    pbc[bb][:, :], ALU.mult).then_inc(vq, 1)

        @block.scalar
        def _(e):
            for h in range(H):
                for g in range(4):
                    un = 4 * h + g
                    bb = un % 2
                    e.wait_ge(mmq, 2 * un + 1)
                    if un >= 2:
                        e.wait_ge(vq, 1 + 2 * (un - 2) + 2)
                    e.activation(araw[bb][:, :], pso[bb][0:64, :], AF.Identity).then_inc(aq, 1)

        @block.tensor
        def _(e):
            e.wait_ge(vq, 1)
            e.wait_ge(ld, 16 * 8)
            for h in range(H):
                hp, po = h // 2, (h % 2) * 64
                for g in range(4):
                    un = 4 * h + g
                    bb = un % 2
                    if un >= 2:
                        e.wait_ge(vq, 1 + 2 * (un - 2) + 2)
                    e.matmul(pso[bb][:, :], kvb16[po:po + 64, (hp % 4) * 65 + (260 if hp >= 4 else 0):
                                                  (hp % 4) * 65 + (260 if hp >= 4 else 0) + 65],
                             qfs[po:po + 64, hp * TOK + 512 * g: hp * TOK + 512 * (g + 1)],
                             start=True, stop=True).then_inc(mmq, 1)
                    e.wait_ge(vq, 1 + 2 * un + 1)
                    e.matmul(pbc[bb][:, :], ons[64:65, 0:64], rcw[64:65, :],
                             start=True, stop=True).then_inc(mmq, 1)

        @block.gpsimd
        def _(e):
            for h in range(H):
                e.wait_ge(vq, 1 + 2 * (4 * h + 3) + 2)
                e.dma_start(out=af1[h, :, :], in_=afh[h % 2][:, :]).then_inc(so[h % 2], 16)
            e.wait_ge(so0, 16 * (H // 2))
            e.wait_ge(so1, 16 * (H // 2))
    return nc


def build_L1E():
    nc = new_nc()
    xqf = nc.dram_tensor("xqf", [D, TOK], BF16, kind="ExternalInput").ap()
    wgT_ = nc.dram_tensor("wgT_", [D, INNER], F32, kind="ExternalInput").ap()
    wuT = nc.dram_tensor("wuT", [D, INNER], F32, kind="ExternalInput").ap()
    sxi = nc.dram_tensor("sxi", [16, 128], F32, kind="ExternalInput").ap()
    ident = nc.dram_tensor("ident", [128, 128], F32, kind="ExternalInput").ap()
    h2q = nc.dram_tensor("h2q", [TOK, INNER], BF16, kind="ExternalOutput").ap()
    sh2i = nc.dram_tensor("sh2i", [16, 128], F32, kind="ExternalOutput").ap()
    wqg_s = nc.dram_tensor("wqg_s", [8, 128, INNER], BF16, kind="Internal").ap()
    wqu_s = nc.dram_tensor("wqu_s", [8, 128, INNER], BF16, kind="Internal").ap()
    h2sc = nc.dram_tensor("h2sc", [16, 128, INNER], F32, kind="Internal").ap()

    NT = 16
    xqs = nc.alloc_sbuf_tensor("xqs", [128, 8 * TOK], BF16).ap()
    wch = [nc.alloc_sbuf_tensor(f"wch{i}", [128, INNER], F32).ap() for i in range(2)]
    wqc = [nc.alloc_sbuf_tensor(f"wqc{i}", [128, INNER], BF16).ap() for i in range(2)]
    qt_ = [nc.alloc_sbuf_tensor(f"qt_{i}", [128, INNER], F32).ap() for i in range(2)]
    rs = nc.alloc_sbuf_tensor("rs", [128, 24], F32).ap()
    ids = nc.alloc_sbuf_tensor("ids", [128, 128], F32).ap()
    sxc = nc.alloc_sbuf_tensor("sxc", [16, 128], F32).ap()
    scols = nc.alloc_sbuf_tensor("scols", [128, 40], F32).ap()
    scg16 = nc.alloc_sbuf_tensor("scg16", [128, 16], F32).ap()
    scu16 = nc.alloc_sbuf_tensor("scu16", [128, 16], F32).ap()
    onef = nc.alloc_sbuf_tensor("onef", [1, 128], F32).ap()
    tot = nc.alloc_sbuf_tensor("tot", [1, 8], F32).ap()
    wgc = [nc.alloc_sbuf_tensor(f"wgc{i}", [128, 8 * 512], BF16).ap() for i in range(2)]
    wuc = [nc.alloc_sbuf_tensor(f"wuc{i}", [128, 8 * 512], BF16).ap() for i in range(2)]
    sgb = [nc.alloc_sbuf_tensor(f"sgb{i}", [128, 512], F32).ap() for i in range(2)]
    tub = [nc.alloc_sbuf_tensor(f"tub{i}", [128, 512], F32).ap() for i in range(2)]
    h2b = [nc.alloc_sbuf_tensor(f"h2b{i}", [128, 512], F32).ap() for i in range(2)]
    h2r = wch
    h2o = [nc.alloc_sbuf_tensor(f"h2o{i}", [128, INNER], BF16).ap() for i in range(2)]
    qq = qt_[0]
    sh16 = nc.alloc_sbuf_tensor("sh16", [16, 128], F32).ap()

    psg = [nc.alloc_psum_tensor(f"psg{i}", [128, 512], F32).ap() for i in range(2)]
    psu = [nc.alloc_psum_tensor(f"psu{i}", [128, 512], F32).ap() for i in range(2)]
    pst = nc.alloc_psum_tensor("pst", [128, 128], F32).ap()

    def v_abs(m, c):
        return 18 * m + c + 1

    def v_csum(m):
        return 18 * m + 9

    def v_stat(m):
        return 18 * m + 10

    def v_qnt(m, c):
        return 18 * m + 10 + c + 1

    V_P1 = 36
    V_SC = V_P1 + 1

    def v_tu(fc, tc):
        return V_SC + (fc * NT + tc) * 2 + 1

    def v_h2(fc, tc):
        return V_SC + (fc * NT + tc) * 2 + 2

    V_P2 = V_SC + 8 * NT * 2

    def v_rm(tc):
        return V_P2 + 3 * tc + 1

    def v_shc(tc):
        return V_P2 + 3 * tc + 2

    def v_h2q(tc):
        return V_P2 + 3 * tc + 3

    V_P3 = V_P2 + 3 * NT + 1

    def m_tr(m):
        return 2 * m + 1

    def m_bsw(m):
        return 2 * m + 2

    M_SX = 5
    M_BCG, M_BCU = 6, 7

    def m_g(fc, tc):
        return M_BCU + (fc * NT + tc) * 2 + 1

    def m_u(fc, tc):
        return M_BCU + (fc * NT + tc) * 2 + 2

    M_P2 = M_BCU + 8 * NT * 2
    M_SH = M_P2 + 1

    def a_sil(fc, tc):
        return fc * NT + tc + 1

    with (
        nc.semaphore("lda") as lda,
        nc.semaphore("ldb") as ldb,
        nc.semaphore("ldx") as ldx,
        nc.semaphore("ldw0") as ldw0,
        nc.semaphore("ldw1") as ldw1,
        nc.semaphore("ldi") as ldi,
        nc.semaphore("mmq") as mmq,
        nc.semaphore("aq") as aq,
        nc.semaphore("vq") as vq,
        nc.semaphore("msd") as msd,
        nc.semaphore("vch") as vch,
        nc.semaphore("soa") as soa,
        nc.semaphore("sob") as sob,
        nc.semaphore("soc") as soc,
        nc.semaphore("sod") as sod,
        nc.Block() as block,
    ):
        lds = [lda, ldb]
        soab = [soa, sob]
        socd = [soc, sod]

        @block.sync
        def _(e):
            e.dma_start(out=ids[:, :], in_=ident[:, :]).then_inc(ldi, 16)
            e.dma_start(out=sxc[:, :], in_=sxi[:, :]).then_inc(ldi, 16)
            for m, wt in ((0, wgT_), (1, wuT)):
                for ps in range(2):
                    for c in range(8):
                        pidx = (m * 2 + ps) * 8 + c
                        if pidx >= 2:
                            pm = (pidx - 2) // 16
                            pps = ((pidx - 2) % 16) // 8
                            pc = (pidx - 2) % 8
                            e.wait_ge(vq, v_abs(pm, pc) if pps == 0 else v_qnt(pm, pc))
                        e.dma_start(out=wch[c % 2][:, :], in_=wt[128 * c:128 * (c + 1), :]) \
                            .then_inc(lds[c % 2], 16)
            for fc in range(8):
                for tc in range(NT):
                    e.wait_ge(vq, v_h2(fc, tc))
                    e.dma_start(out=h2sc[tc, :, 512 * fc:512 * (fc + 1)],
                                in_=h2b[(fc * NT + tc) % 2][:, :]).then_inc(socd[tc % 2], 16)
            e.wait_ge(soc, 16 * 64)
            e.wait_ge(sod, 16 * 64)
            for tc in range(NT):
                if tc >= 2:
                    e.wait_ge(vq, v_rm(tc - 2) + 2)
                e.dma_start(out=h2r[tc % 2][:, :], in_=h2sc[tc, :, :]) \
                    .then_inc(lds[tc % 2], 16)

        @block.gpsimd
        def _(e):
            for c in range(8):
                e.dma_start(out=xqs[:, c * TOK:(c + 1) * TOK],
                            in_=xqf[128 * c:128 * (c + 1), :]).then_inc(ldx, 16)
            for m in range(2):
                sc_ = (wqg_s, wqu_s)[m]
                for c in range(8):
                    e.wait_ge(vq, v_qnt(m, c))
                    e.dma_start(out=sc_[c, :, :], in_=wqc[c % 2][:, :]) \
                        .then_inc(soab[c % 2], 16)
            ldw = [ldw0, ldw1]
            e.wait_ge(soa, 16 * 8)
            e.wait_ge(sob, 16 * 8)
            for fc in range(8):
                if fc >= 2:
                    e.wait_ge(mmq, m_u(fc - 2, NT - 1))
                for dc in range(8):
                    e.dma_start(out=wgc[fc % 2][:, 512 * dc:512 * (dc + 1)],
                                in_=wqg_s[dc, :, 512 * fc:512 * (fc + 1)]).then_inc(ldw[fc % 2], 16)
                    e.dma_start(out=wuc[fc % 2][:, 512 * dc:512 * (dc + 1)],
                                in_=wqu_s[dc, :, 512 * fc:512 * (fc + 1)]).then_inc(ldw[fc % 2], 16)
            for tc in range(NT):
                e.wait_ge(vq, v_h2q(tc))
                e.dma_start(out=h2q[128 * tc:128 * (tc + 1), :],
                            in_=h2o[tc % 2][:, :]).then_inc(soab[tc % 2], 16)
            e.wait_ge(vq, V_P3)
            e.dma_start(out=sh2i[:, :], in_=sh16[:, :]).then_inc(soa, 16)
            e.wait_ge(soa, 16 * (8 + 8 + 1))
            e.wait_ge(sob, 16 * (8 + 8))
            e.wait_ge(soc, 16 * 64)
            e.wait_ge(sod, 16 * 64)

        @block.tensor
        def _(e):
            e.wait_ge(msd, 1)
            e.wait_ge(ldi, 32)
            for m in range(2):
                e.wait_ge(vq, v_csum(m))
                e.transpose(pst[0:1, 0:128], rs[:, 16 + m:17 + m], ids[:, :]).then_inc(mmq, 1)
                e.wait_ge(vq, v_stat(m))
                e.matmul(psu[m][:, 0:1], onef[0:1, :], tot[0:1, 4 + m:5 + m],
                         start=True, stop=True).then_inc(mmq, 1)
            e.wait_ge(ldi, 32)
            e.transpose(pst[0:128, 0:16], sxc[0:16, 0:128], ids[0:16, 0:16]).then_inc(mmq, 1)
            e.matmul(psg[0][:, 0:1], onef[0:1, :], tot[0:1, 0:1], start=True, stop=True) \
                .then_inc(mmq, 1)
            e.matmul(psg[1][:, 0:1], onef[0:1, :], tot[0:1, 1:2], start=True, stop=True) \
                .then_inc(mmq, 1)
            e.wait_ge(ldx, 16 * 8)
            e.wait_ge(vq, V_SC)
            ldw = [ldw0, ldw1]
            for fc in range(8):
                e.wait_ge(ldw[fc % 2], 16 * 16 * (fc // 2 + 1))
                for tc in range(NT):
                    pj = fc * NT + tc
                    bb = pj % 2
                    if pj >= 2:
                        e.wait_ge(aq, a_sil(*divmod(pj - 2, NT)))
                        e.wait_ge(vq, v_tu(*divmod(pj - 2, NT)))
                    for dc in range(8):
                        inst = e.matmul(psg[bb][:, :],
                                        xqs[:, dc * TOK + 128 * tc: dc * TOK + 128 * (tc + 1)],
                                        wgc[fc % 2][:, 512 * dc:512 * (dc + 1)],
                                        start=(dc == 0), stop=(dc == 7))
                    inst.then_inc(mmq, 1)
                    for dc in range(8):
                        inst = e.matmul(psu[bb][:, :],
                                        xqs[:, dc * TOK + 128 * tc: dc * TOK + 128 * (tc + 1)],
                                        wuc[fc % 2][:, 512 * dc:512 * (dc + 1)],
                                        start=(dc == 0), stop=(dc == 7))
                    inst.then_inc(mmq, 1)
            e.wait_ge(vq, V_P2 + 3 * NT)
            e.transpose(pst[0:16, 0:128], scols[:, 20:36], ids[:, :]).then_inc(mmq, 1)

        @block.scalar
        def _(e):
            for fc in range(8):
                for tc in range(NT):
                    pj = fc * NT + tc
                    bb = pj % 2
                    e.wait_ge(mmq, m_g(fc, tc))
                    e.wait_ge(vq, V_SC)
                    if pj >= 2:
                        e.wait_ge(vq, v_h2(*divmod(pj - 2, NT)))
                    e.activation(sgb[bb][:, :], psg[bb][:, :], AF.Silu,
                                 scale=scg16[:, tc:tc + 1]).then_inc(aq, 1)

        @block.vector
        def _(e):
            cnt = [0]
            e.memset(onef[:, :], 1.0).then_inc(msd, 1)
            for m in range(2):
                for c in range(8):
                    e.wait_ge(lds[c % 2], 16 * (8 * m + c // 2 + 1))
                    e.tensor_reduce(rs[:, 8 * m + c:8 * m + c + 1], wch[c % 2][:, :],
                                    AX.X, ALU.add, apply_absolute_value=True).then_inc(vq, 1)
                e.wait_ge(vq, v_abs(m, 7))
                e.tensor_reduce(rs[:, 16 + m:17 + m], rs[:, 8 * m:8 * m + 8], AX.X, ALU.add) \
                    .then_inc(vq, 1)
                e.wait_ge(mmq, m_tr(m))
                fence(e, vch, cnt, e.tensor_reduce(tot[0:1, m:m + 1], pst[0:1, 0:128],
                                                   AX.X, ALU.add))
                fence(e, vch, cnt, e.tensor_scalar(tot[0:1, m:m + 1], tot[0:1, m:m + 1],
                                                   1.0 / (D * INNER), None, ALU.mult))
                fence(e, vch, cnt, e.tensor_scalar_max(tot[0:1, m:m + 1], tot[0:1, m:m + 1],
                                                       1e-5))
                fence(e, vch, cnt, e.reciprocal(tot[0:1, 4 + m:5 + m], tot[0:1, m:m + 1]))
                e.engine_nop().then_inc(vq, 1)
                e.wait_ge(mmq, m_bsw(m))
                fence(e, vch, cnt, e.tensor_copy(scols[:, 18 + m:19 + m], psu[m][:, 0:1]))
                for c in range(8):
                    bb = c % 2
                    e.wait_ge(lds[bb], 16 * (8 * m + 4 + c // 2 + 1))
                    if 8 * m + c >= 2:
                        pi = 8 * m + c - 2
                        e.wait_ge(soab[bb], 16 * (pi // 2 + 1))
                    e.tensor_scalar(qt_[bb][:, :], wch[bb][:, :], scols[:, 18 + m:19 + m],
                                    MAGIC, ALU.mult, ALU.add)
                    fence(e, vch, cnt, e.engine_nop())
                    e.tensor_scalar(qt_[bb][:, :], qt_[bb][:, :], -MAGIC, 1.0,
                                    ALU.add, ALU.min)
                    fence(e, vch, cnt, e.engine_nop())
                    e.tensor_scalar_max(wqc[bb][:, :], qt_[bb][:, :], -1.0).then_inc(vq, 1)
            e.wait_ge(mmq, M_SX)
            fence(e, vch, cnt, e.tensor_copy(scols[:, 0:16], pst[0:128, 0:16]))
            e.wait_ge(mmq, M_BCU)
            fence(e, vch, cnt, e.tensor_copy(scols[:, 16:17], psg[0][:, 0:1]))
            fence(e, vch, cnt, e.tensor_copy(scols[:, 17:18], psg[1][:, 0:1]))
            fence(e, vch, cnt, e.tensor_scalar(scg16[:, :], scols[:, 0:16],
                                               scols[:, 16:17], None, ALU.mult))
            e.tensor_scalar(scu16[:, :], scols[:, 0:16], scols[:, 17:18], None, ALU.mult) \
                .then_inc(vq, 1)
            for fc in range(8):
                for tc in range(NT):
                    pj = fc * NT + tc
                    bb = pj % 2
                    e.wait_ge(mmq, m_u(fc, tc))
                    e.tensor_scalar(tub[bb][:, :], psu[bb][:, :], scu16[:, tc:tc + 1],
                                    None, ALU.mult).then_inc(vq, 1)
                    e.wait_ge(vq, v_tu(fc, tc))
                    e.wait_ge(aq, a_sil(fc, tc))
                    if pj >= 2:
                        e.wait_ge(socd[pj % 2], 16 * ((pj - 2) // 2 + 1))
                    e.tensor_tensor(h2b[bb][:, :], sgb[bb][:, :], tub[bb][:, :],
                                    ALU.mult).then_inc(vq, 1)
            for tc in range(NT):
                bb = tc % 2
                e.wait_ge(lds[bb], 16 * (16 + tc // 2 + 1))
                fence(e, vch, cnt,
                      e.tensor_reduce(scols[:, 36:37], h2r[bb][:, :], AX.X, ALU.max,
                                      apply_absolute_value=True))
                fence(e, vch, cnt, e.tensor_scalar_max(scols[:, 36:37], scols[:, 36:37], 1e-5))
                fence(e, vch, cnt, e.reciprocal(scols[:, 37:38], scols[:, 36:37]))
                fence(e, vch, cnt, e.tensor_scalar(scols[:, 37:38], scols[:, 37:38],
                                                   127.0, None, ALU.mult))
                e.engine_nop().then_inc(vq, 1)
                e.tensor_scalar(scols[:, 20 + tc:21 + tc], scols[:, 36:37],
                                1.0 / 127.0, None, ALU.mult).then_inc(vq, 1)
                if tc >= 2:
                    e.wait_ge(soab[bb], 16 * (8 + (tc - 2) // 2 + 1))
                e.tensor_scalar(qq[:, :], h2r[bb][:, :], scols[:, 37:38], MAGIC,
                                ALU.mult, ALU.add)
                fence(e, vch, cnt, e.engine_nop())
                e.tensor_scalar(qq[:, :], qq[:, :], -MAGIC, 127.0, ALU.add, ALU.min)
                fence(e, vch, cnt, e.engine_nop())
                e.tensor_scalar_max(h2o[bb][:, :], qq[:, :], -128.0).then_inc(vq, 1)
            e.wait_ge(mmq, M_SH)
            e.tensor_copy(sh16[:, :], pst[0:16, 0:128]).then_inc(vq, 1)
    return nc



def build_L1F():
    nc = new_nc()
    h2qf = nc.dram_tensor("h2qf", [16, 128, 32 * 128], BF16, kind="ExternalInput").ap()
    wdT = nc.dram_tensor("wdT", [INNER, D], F32, kind="ExternalInput").ap()
    sh2i = nc.dram_tensor("sh2i", [16, 128], F32, kind="ExternalInput").ap()
    x2p = nc.dram_tensor("x2p", [TOK, D], F32, kind="ExternalInput").ap()
    ident = nc.dram_tensor("ident", [128, 128], F32, kind="ExternalInput").ap()
    x2 = nc.dram_tensor("x2", [TOK, D], F32, kind="ExternalOutput").ap()

    NT = 16
    wds = nc.alloc_sbuf_tensor("wds", [128, 32 * D], BF16).ap()
    wch = [nc.alloc_sbuf_tensor(f"wch{i}", [128, D], F32).ap() for i in range(2)]
    qt_ = [nc.alloc_sbuf_tensor(f"qt_{i}", [128, D], F32).ap() for i in range(2)]
    h2t = [nc.alloc_sbuf_tensor(f"h2t{i}", [128, 32 * 128], BF16).ap() for i in range(2)]
    xps = [nc.alloc_sbuf_tensor(f"xps{i}", [128, D], F32).ap() for i in range(2)]
    x2s = [nc.alloc_sbuf_tensor(f"x2s{i}", [128, D], F32).ap() for i in range(2)]
    rs = nc.alloc_sbuf_tensor("rs", [128, 36], F32).ap()
    ids = nc.alloc_sbuf_tensor("ids", [128, 128], F32).ap()
    shc = nc.alloc_sbuf_tensor("shc", [16, 128], F32).ap()
    scd = nc.alloc_sbuf_tensor("scd", [128, 18], F32).ap()
    onef = nc.alloc_sbuf_tensor("onef", [1, 128], F32).ap()
    tot = nc.alloc_sbuf_tensor("tot", [1, 4], F32).ap()

    pso = [nc.alloc_psum_tensor(f"pso{i}", [128, 512], F32).ap() for i in range(4)]
    pst = nc.alloc_psum_tensor("pst", [128, 128], F32).ap()

    def v_abs(c):
        return c + 1

    V_CS = 33
    V_ST = 34

    def v_qnt(c):
        return 34 + c + 1

    V_P1 = 66
    V_SC = V_P1 + 1
    def v_x2(tc):
        return V_SC + tc + 1

    def m_o(tc, oc):
        return 4 + 2 * tc + oc + 1

    with (
        nc.semaphore("lda") as lda,
        nc.semaphore("ldb") as ldb,
        nc.semaphore("ldx") as ldx,
        nc.semaphore("ldh0") as ldh0,
        nc.semaphore("ldh1") as ldh1,
        nc.semaphore("mmq") as mmq,
        nc.semaphore("vq") as vq,
        nc.semaphore("msd") as msd,
        nc.semaphore("vch") as vch,
        nc.semaphore("so0") as so0,
        nc.semaphore("so1") as so1,
        nc.Block() as block,
    ):
        lds = [lda, ldb]
        ldh = [ldh0, ldh1]
        so = [so0, so1]

        @block.sync
        def _(e):
            e.dma_start(out=ids[:, :], in_=ident[:, :]).then_inc(ldx, 16)
            e.dma_start(out=shc[:, :], in_=sh2i[:, :]).then_inc(ldx, 16)
            for ps in range(2):
                for c in range(32):
                    pidx = ps * 32 + c
                    if pidx >= 2:
                        pc = (pidx - 2) % 32
                        e.wait_ge(vq, v_abs(pc) if (pidx - 2) < 32 else v_qnt(pc))
                    e.dma_start(out=wch[c % 2][:, :], in_=wdT[128 * c:128 * (c + 1), :]) \
                        .then_inc(lds[c % 2], 16)

        @block.gpsimd
        def _(e):
            for tc in range(NT + 2):
                if tc >= 2:
                    e.wait_ge(vq, v_x2(tc - 2))
                    e.dma_start(out=x2[128 * (tc - 2):128 * (tc - 1), :],
                                in_=x2s[tc % 2][:, :]).then_inc(so[tc % 2], 16)
                if tc < NT:
                    if tc >= 2:
                        e.wait_ge(mmq, m_o(tc - 2, 1))
                    e.dma_start(out=h2t[tc % 2][:, :], in_=h2qf[tc, :, :]) \
                        .then_inc(ldh[tc % 2], 16)
                    e.dma_start(out=xps[tc % 2][:, :], in_=x2p[128 * tc:128 * (tc + 1), :]) \
                        .then_inc(ldh[tc % 2], 16)
            e.wait_ge(so0, 16 * 8)
            e.wait_ge(so1, 16 * 8)

        @block.tensor
        def _(e):
            e.wait_ge(vq, V_CS)
            e.wait_ge(ldx, 32)
            e.transpose(pst[0:1, 0:128], rs[:, 32:33], ids[:, :]).then_inc(mmq, 1)
            e.wait_ge(vq, V_ST)
            e.transpose(pst[0:128, 0:16], shc[0:16, 0:128], ids[0:16, 0:16]).then_inc(mmq, 1)
            e.wait_ge(msd, 1)
            e.matmul(pso[3][:, 0:1], onef[0:1, :], tot[0:1, 1:2], start=True, stop=True) \
                .then_inc(mmq, 1)
            e.matmul(pso[3][:, 1:2], onef[0:1, :], tot[0:1, 0:1], start=True, stop=True) \
                .then_inc(mmq, 1)
            e.wait_ge(vq, V_SC)
            for tc in range(NT):
                e.wait_ge(ldh[tc % 2], 32 * (tc // 2 + 1))
                for oc in range(2):
                    bb = 2 * (tc % 2) + oc
                    if tc >= 2:
                        e.wait_ge(vq, v_x2(tc - 2))
                    for fcp in range(32):
                        inst = e.matmul(pso[bb][:, :],
                                        h2t[tc % 2][:, 128 * fcp:128 * (fcp + 1)],
                                        wds[:, fcp * D + 512 * oc: fcp * D + 512 * (oc + 1)],
                                        start=(fcp == 0), stop=(fcp == 31))
                    inst.then_inc(mmq, 1)

        @block.vector
        def _(e):
            cnt = [0]
            e.memset(onef[:, :], 1.0).then_inc(msd, 1)
            for c in range(32):
                e.wait_ge(lds[c % 2], 16 * (c // 2 + 1))
                e.tensor_reduce(rs[:, c:c + 1], wch[c % 2][:, :],
                                AX.X, ALU.add, apply_absolute_value=True).then_inc(vq, 1)
            e.wait_ge(vq, 32)
            e.tensor_reduce(rs[:, 32:33], rs[:, 0:32], AX.X, ALU.add).then_inc(vq, 1)
            e.wait_ge(mmq, 1)
            fence(e, vch, cnt, e.tensor_reduce(tot[0:1, 0:1], pst[0:1, 0:128], AX.X, ALU.add))
            fence(e, vch, cnt, e.tensor_scalar(tot[0:1, 0:1], tot[0:1, 0:1],
                                               1.0 / (INNER * D), None, ALU.mult))
            fence(e, vch, cnt, e.tensor_scalar_max(tot[0:1, 0:1], tot[0:1, 0:1], 1e-5))
            fence(e, vch, cnt, e.reciprocal(tot[0:1, 1:2], tot[0:1, 0:1]))
            e.engine_nop().then_inc(vq, 1)
            e.wait_ge(mmq, 3)
            fence(e, vch, cnt, e.tensor_copy(scd[:, 16:17], pso[3][:, 0:1]))
            for c in range(32):
                bb = c % 2
                e.wait_ge(lds[bb], 16 * (16 + c // 2 + 1))
                e.tensor_scalar(qt_[bb][:, :], wch[bb][:, :], scd[:, 16:17], MAGIC,
                                ALU.mult, ALU.add)
                fence(e, vch, cnt, e.engine_nop())
                e.tensor_scalar(qt_[bb][:, :], qt_[bb][:, :], -MAGIC, 1.0, ALU.add, ALU.min)
                fence(e, vch, cnt, e.engine_nop())
                e.tensor_scalar_max(wds[:, c * D:(c + 1) * D], qt_[bb][:, :], -1.0) \
                    .then_inc(vq, 1)
            e.wait_ge(mmq, 2)
            fence(e, vch, cnt, e.tensor_copy(scd[:, 0:16], pst[0:128, 0:16]))
            e.wait_ge(mmq, 4)
            fence(e, vch, cnt, e.tensor_copy(scd[:, 17:18], pso[3][:, 1:2]))
            e.tensor_scalar(scd[:, 0:16], scd[:, 0:16], scd[:, 17:18], None, ALU.mult) \
                .then_inc(vq, 1)
            for tc in range(NT):
                bb = tc % 2
                e.wait_ge(mmq, m_o(tc, 1))
                e.wait_ge(ldh[bb], 32 * (tc // 2 + 1))
                if tc >= 2:
                    e.wait_ge(so[bb], 16 * (tc // 2))
                for oc in range(2):
                    inst = e.scalar_tensor_tensor(
                        x2s[bb][:, 512 * oc:512 * (oc + 1)],
                        pso[2 * (tc % 2) + oc][:, :], scd[:, tc:tc + 1],
                        xps[bb][:, 512 * oc:512 * (oc + 1)], ALU.mult, ALU.add)
                inst.then_inc(vq, 1)
    return nc



_NCS = {}


def _get_nc(name):
    if name not in _NCS:
        _NCS[name] = globals()["build_" + name]()
    return _NCS[name]


def _launch(name, in_maps):
    res = run_bass_kernel_spmd(_get_nc(name), in_maps, core_ids=list(range(N_CORES)))
    return res.results


def kernel(L0_attn_x, L0_attn_W_Q, L0_attn_W_K, L0_attn_W_V, L0_attn_W_O,
           L0_ffn_Wg, L0_ffn_bg, L0_ffn_w1, L0_ffn_w2,
           L1_attn_w_qkv, L1_attn_w_out,
           L1_ffn_w_gate, L1_ffn_w_up, L1_ffn_w_down):
    x = np.asarray(L0_attn_x, dtype=np.float32)
    cos, sin = _rope_tables()
    sin2 = sin.copy()
    sin2[:, :32] *= -1.0
    ident = np.eye(128, dtype=np.float32)
    tri = np.tril(np.ones((128, 128), np.float32)).T.astype(BF)

    wqT = _t(L0_attn_W_Q)
    wkT = _t(L0_attn_W_K)
    wvT = _t(L0_attn_W_V)
    woT = _t(L0_attn_W_O)
    wgT = _t(L0_ffn_Wg)
    bgv = np.ascontiguousarray(np.asarray(L0_ffn_bg, np.float32).reshape(E, 1))
    w1T = _cs(np.asarray(L0_ffn_w1, np.float32).transpose(0, 2, 1))
    w2T = _cs(np.asarray(L0_ffn_w2, np.float32).transpose(0, 2, 1))
    wqkvT = _t(L1_attn_w_qkv)
    woutT = _t(L1_attn_w_out)
    wgT_ = _t(L1_ffn_w_gate)
    wuT = _t(L1_ffn_w_up)
    wdT = _t(L1_ffn_w_down)

    own_loc = np.concatenate(
        [np.arange(128 * (2 * ql + 1), 128 * (2 * ql + 1) + 128) for ql in range(QL)])
    pars = [c % 2 for c in range(N_CORES)]
    seqs = [c // 2 for c in range(N_CORES)]
    keypos = {}
    mrow = {}
    keypos[1] = np.arange(S)
    keypos[0] = np.concatenate([np.zeros(128, np.int64), np.arange(S - 128)])
    mrow[1] = np.zeros(S, np.float32)
    mrow[0] = np.concatenate([np.full(128, -1e5, np.float32), np.zeros(S - 128, np.float32)])
    cosq = {p: np.tile(cos[keypos[p][own_loc]], (1, H)).astype(BF) for p in (0, 1)}
    sinq = {p: np.tile(sin2[keypos[p][own_loc]], (1, H)).astype(BF) for p in (0, 1)}
    cosk = {p: np.tile(cos[keypos[p]], (1, H)).astype(BF) for p in (0, 1)}
    sink = {p: np.tile(sin2[keypos[p]], (1, H)).astype(BF) for p in (0, 1)}

    def xk_of(c):
        xs = x[seqs[c]]
        if pars[c] == 1:
            return xs
        return np.concatenate([np.zeros((128, D), np.float32), xs[:S - 128]])

    r = _launch("L0A", [{"xk": _cs(xk_of(c))} for c in range(N_CORES)])
    xn = [r[c]["xn"] for c in range(N_CORES)]
    xn32 = [r[c]["xn32o"] for c in range(N_CORES)]

    r = _launch("L0B", [{
        "xnf": _cs(xn[c].T), "wqT": wqT, "wkT": wkT, "wvT": wvT,
        "cosq": cosq[pars[c]], "sinq": sinq[pars[c]],
        "cosk": cosk[pars[c]], "sink": sink[pars[c]]} for c in range(N_CORES)])
    qo = [r[c]["qo"] for c in range(N_CORES)]
    ko = [r[c]["ko"] for c in range(N_CORES)]
    vo = [r[c]["vo"] for c in range(N_CORES)]

    ims = []
    for c in range(N_CORES):
        q, k, v = qo[c], ko[c], vo[c]
        qhv = np.zeros((H, 65, TOK), BF)
        khv = np.zeros((H, 65, S), BF)
        vhv = np.zeros((H, 128, SB * 65), BF)
        mr = mrow[pars[c]].astype(BF)
        for h in range(H):
            qhv[h, :64] = q[:, h * 64:(h + 1) * 64].T
            qhv[h, 64] = BF(8.0)
            khv[h, :64] = k[:, h * 64:(h + 1) * 64].T
            khv[h, 64] = mr
            va = np.concatenate([v[:, h * 64:(h + 1) * 64].astype(BF),
                                 np.ones((S, 1), BF)], 1)
            vhv[h] = va.reshape(SB, 128, 65).transpose(1, 0, 2).reshape(128, SB * 65)
        ims.append({"qh": qhv, "kh": khv, "vh": vhv, "tri": tri})
    r = _launch("L0C", ims)
    af = [r[c]["af"] for c in range(N_CORES)]

    r = _launch("L0M", [{
        "xnf": _cs(xn[c][own_loc].T), "xnf32": _cs(xn32[c].T),
        "wgT": wgT, "bg": bgv, "w1T": w1T, "w2T": w2T, "ident": ident}
        for c in range(N_CORES)])
    moe = [r[c]["moe"] for c in range(N_CORES)]

    r = _launch("L0D", [{
        "attnf": _cs(af[c].reshape(D, TOK)), "woT": woT,
        "xo": _cs(x[seqs[c]][keypos[pars[c]][own_loc]]),
        "moet": _cs(moe[c].T)} for c in range(N_CORES)])
    x1 = [r[c]["x1"] for c in range(N_CORES)]

    r = _launch("L1A", [{"x1": x1[c], "ident": ident} for c in range(N_CORES)])
    xn1 = [r[c]["xn1"] for c in range(N_CORES)]
    xq = [r[c]["xq"] for c in range(N_CORES)]
    sxi = [r[c]["sxi"] for c in range(N_CORES)]

    r = _launch("L1B", [{"xn1f": _cs(xn1[c].T), "wqkvT": wqkvT} for c in range(N_CORES)])
    qf = [r[c]["qf"] for c in range(N_CORES)]
    kvp = [r[c]["kvp"] for c in range(N_CORES)]

    r = _launch("L1C", [{"kv0": kvp[c], "kv1": kvp[c ^ 1], "qfi": qf[c]}
                        for c in range(N_CORES)])
    af1 = [r[c]["af1"] for c in range(N_CORES)]

    zmoe = np.zeros((TOK, D), np.float32)
    r = _launch("L0D", [{"attnf": _cs(af1[c].reshape(D, TOK)), "woT": woutT,
                         "xo": x1[c], "moet": zmoe} for c in range(N_CORES)])
    x2p = [r[c]["x1"] for c in range(N_CORES)]

    r = _launch("L1E", [{"xqf": _cs(xq[c].T), "wgT_": wgT_, "wuT": wuT,
                         "sxi": sxi[c], "ident": ident} for c in range(N_CORES)])
    h2q = [r[c]["h2q"] for c in range(N_CORES)]
    sh2i = [r[c]["sh2i"] for c in range(N_CORES)]

    ims = []
    for c in range(N_CORES):
        h2qf = np.zeros((16, 128, 32 * 128), BF)
        for tc in range(16):
            a = h2q[c][tc * 128:(tc + 1) * 128, :]
            h2qf[tc] = a.reshape(128, 32, 128).transpose(2, 1, 0).reshape(128, 32 * 128)
        ims.append({"h2qf": h2qf, "wdT": wdT, "sh2i": sh2i[c], "x2p": x2p[c],
                    "ident": ident})
    r = _launch("L1F", ims)

    out = np.zeros((B, S, D), np.float32)
    for c in range(N_CORES):
        out[seqs[c], keypos[pars[c]][own_loc]] = r[c]["x2"]
    return out
